# version 15
# baseline (speedup 1.0000x reference)
import numpy as np
import ml_dtypes
import concourse.bass as bass
import concourse.mybir as mybir
from concourse.bass_utils import run_bass_kernel_spmd
from contextlib import ExitStack

F32 = mybir.dt.float32
BF16 = mybir.dt.bfloat16
I32 = mybir.dt.int32
AL = mybir.AluOpType
AF = mybir.ActivationFunctionType
AX = mybir.AxisListType

T = 4096
D = 1024
H = 16
NT = 32
RIMM = -2.0e30
FLAGT = -1.5e30
NEG = -1.0e30


def _consts():
    tri = np.zeros((128, 128), dtype=np.float32)
    tri[np.triu_indices(128, 1)] = NEG
    j = np.arange(T, dtype=np.float64)
    j_hi = np.floor(j / 32.0) * 32.0
    j_lo = j - j_hi
    j4 = np.stack([j_hi, j_lo, j_hi, j_lo]).astype(np.float32)
    slopes = (2.0 ** (-(np.arange(H, dtype=np.float64) / (H - 1)) * 8.0)).astype(np.float32)
    s_hi = slopes.astype(ml_dtypes.bfloat16).astype(np.float32)
    s_lo = (slopes - s_hi).astype(np.float32)
    sl4 = np.zeros((4, H * 128), dtype=np.float32)
    for h in range(H):
        sl4[0, h * 128:(h + 1) * 128] = s_hi[h]
        sl4[1, h * 128:(h + 1) * 128] = s_hi[h]
        sl4[2, h * 128:(h + 1) * 128] = s_lo[h]
        sl4[3, h * 128:(h + 1) * 128] = s_lo[h]
    icol = np.arange(128, dtype=np.float32).reshape(128, 1)
    return (tri, j4.astype(ml_dtypes.bfloat16), sl4.astype(ml_dtypes.bfloat16),
            np.eye(128, dtype=ml_dtypes.bfloat16), icol)


class P:

    def __init__(self, nc, sems):
        self.nc = nc
        self.sems = sems
        self.cnt = None
        self.eng_name = None
        self.eng = None
        self.last_wait = None
        self.tags = {}
        self.eng_pc = 0

    def begin(self, eng_name, eng):
        self.eng_name = eng_name
        self.eng = eng
        self.cnt = {k: 0 for k in self.sems}
        self.last_wait = {k: 0 for k in self.sems}
        self.tags[eng_name] = []
        self.eng_pc = 0

    def op(self, eng, fn, waits=(), incs=(), dma=False):
        if eng == self.eng_name:
            import sys as _sys
            _ln = _sys._getframe(1).f_lineno
            _i0 = self.nc.n_instructions()
            for w in waits:
                if isinstance(w, tuple):
                    w, v = w
                else:
                    v = self.cnt[w]
                if v > 0 and v > self.last_wait[w]:
                    self.eng.wait_ge(self.sems[w], v)
                    self.last_wait[w] = v
            inst = fn(self.eng)
            _n = self.nc.n_instructions() - _i0
            self.tags[self.eng_name].append((self.eng_pc, self.eng_pc + _n, _ln))
            self.eng_pc += _n
            amt = 16 if dma else 1
            if incs:
                assert inst is not None
                for s in incs:
                    inst.then_inc(self.sems[s], amt)
        amt = 16 if dma else 1
        for s in incs:
            self.cnt[s] += amt


_last_p = [None]


def build_nc(nt=NT, dbg=False, ph=99, pass_cap=2048):
    Tl = 128 * nt
    nc = bass.Bass("TRN2", target_bir_lowering=False)

    x_d = nc.dram_tensor("x", [Tl, D], F32, kind="ExternalInput").ap()
    y_d = nc.dram_tensor("y", [Tl, D], F32, kind="ExternalOutput").ap()
    wq_d = nc.dram_tensor("wq", [D, D], F32, kind="ExternalInput").ap()
    bq_d = nc.dram_tensor("bq", [D], F32, kind="ExternalInput").ap()
    wkv_d = nc.dram_tensor("wkv", [D, 512], F32, kind="ExternalInput").ap()
    bkv_d = nc.dram_tensor("bkv", [512], F32, kind="ExternalInput").ap()
    wo_d = nc.dram_tensor("wo", [D, D], F32, kind="ExternalInput").ap()
    bo_d = nc.dram_tensor("bo", [D], F32, kind="ExternalInput").ap()
    qn_d = nc.dram_tensor("qn", [64], F32, kind="ExternalInput").ap()
    kn_d = nc.dram_tensor("kn", [64], F32, kind="ExternalInput").ap()
    nw_d = nc.dram_tensor("nw", [D], F32, kind="ExternalInput").ap()
    aw_d = nc.dram_tensor("aw", [D, D], F32, kind="ExternalInput").ap()
    abias_d = nc.dram_tensor("abias", [D], F32, kind="ExternalInput").ap()
    bw_d = nc.dram_tensor("bw", [D, D], F32, kind="ExternalInput").ap()
    cw_d = nc.dram_tensor("cw", [D, D], F32, kind="ExternalInput").ap()
    dw_d = nc.dram_tensor("dw", [D, D], F32, kind="ExternalInput").ap()
    ow_d = nc.dram_tensor("ow", [D, D], F32, kind="ExternalInput").ap()
    obias_d = nc.dram_tensor("obias", [D], F32, kind="ExternalInput").ap()
    fk_d = nc.dram_tensor("fk", [D, D], F32, kind="ExternalInput").ap()
    fb_d = nc.dram_tensor("fb", [D], F32, kind="ExternalInput").ap()
    fa_d = nc.dram_tensor("fa", [D, 16], F32, kind="ExternalInput").ap()
    fbm_d = nc.dram_tensor("fbm", [16, D], F32, kind="ExternalInput").ap()

    tri_c, j4_c, sl4_c, identb_c, icol_c = _consts()
    icol_d = nc.inline_tensor(icol_c, "icol_c").ap()
    tri_d = nc.inline_tensor(tri_c, "tri_c").ap()
    j4_d = nc.inline_tensor(j4_c, "j4_c").ap()
    sl4_d = nc.inline_tensor(sl4_c, "sl4_c").ap()
    identb_d = nc.inline_tensor(identb_c, "identb_c").ap()

    wt_scr = [nc.dram_tensor(f"wtscr{i}", [D, D], BF16, kind="Internal").ap()
              for i in range(4)]
    gscr = nc.dram_tensor("gscr", [128, 1], F32, kind="Internal").ap()
    dbg_t = {}
    if dbg:
        for nm, shp in [("dq", [128, D]), ("dkv", [128, 512]), ("dqh", [128, D]),
                        ("dkh", [128, 256]), ("ds", [128, 128]), ("dwb", [128, 128]),
                        ("dT8", [128, 64]), ("dwn", [128, 128]), ("dctx", [128, D]),
                        ("dres1", [128, D]), ("dres2", [128, D]), ("dres3", [128, D]),
                        ("dffnq", [128, D]), ("dxn2", [128, D]), ("dscl", [128, 2])]:
            dbg_t[nm] = nc.dram_tensor(nm, shp, F32, kind="ExternalOutput").ap()

    es = ExitStack()
    sb = lambda n, s, dt: es.enter_context(nc.sbuf_tensor(n, s, dt))
    with es:
        wqb = sb("wqb", [128, 8 * D], BF16)
        wkvb = sb("wkvb", [128, 8 * 512], BF16)
        wob = sb("wob", [128, 8 * D], BF16)
        ffnq = sb("ffnq", [128, 8 * D], BF16)
        ab = sb("ab", [128, 8 * 16], BF16)
        outwT = sb("outwT", [128, 8 * D], BF16)
        ssw = sb("ssw", [128, 6 * D], BF16)
        kt = [sb("kt0", [128, Tl], BF16), sb("kt1", [128, Tl], BF16)]
        V = sb("V", [128, nt * 256], BF16)
        res = sb("res", [128, 4 * D], F32)
        s_sb = sb("s_sb", [128, max(Tl, 4096)], F32)
        Wb = sb("Wbuf", [128, max(Tl, 4096)], BF16)
        Wn = sb("Wnorm", [128, max(Tl, 4096)], BF16)
        WT = sb("WT", [128, max(Tl, 4096)], BF16)
        qTb = [sb("qT0", [128, D], BF16), sb("qT1", [128, D], BF16)]
        xnT = sb("xnT", [128, D], BF16)
        xn_bf = sb("xn_bf", [128, D], BF16)
        CTXT = sb("CTXT", [128, D], BF16)
        q16 = sb("q16", [128, D], BF16)
        kv16 = sb("kv16", [128, 512], BF16)
        xn3T = WT
        Hp = Wb
        asg = sb("asg", [128, 512], F32)
        dsg = sb("dsg", [128, 512], F32)
        csg = sb("csg", [128, 512], BF16)
        dbt = dsg
        hh = dbt
        QN8B = sb("QN8B", [128, 64], BF16)
        KNB = sb("KNB", [128, 64], BF16)
        tri = sb("tri", [128, 128], F32)
        identb = sb("identb", [128, 128], BF16)
        sh = sb("sh", [128, 6144], BF16)
        j4b = sh[0:4, 0:T]
        sl4 = sh[0:4, 4096:4096 + H * 128]
        onescb = sh[64:65, 5888:6016]
        bqr = sh[64:65, 0:D]
        bor = sh[64:65, 1024:2048]
        fbr = sh[64:65, 2048:3072]
        obr = sh[64:65, 3072:4096]
        bkvr = sh[64:65, 4096:4608]
        qn8r = sh[64:65, 4608:5632]
        knr = sh[64:65, 5632:5888]
        onescf = s_sb[0:1, 3200:3328]
        rowf = s_sb[0:1, 2048:3072]
        rowf2 = s_sb[0:1, 3072:4096]
        fbm32 = s_sb[0:16, 3072:4096]
        abt = sb("abt", [128, 8], F32)
        nwt = sb("nwt", [128, 8], F32)
        T8 = sb("T8", [128, 64], F32)
        sm = sb("sm", [128, 64], F32)
        carry = sb("carry", [128, 8], F32)
        cb = sb("cb", [128, 2], F32)
        lb0 = sb("lb0", [16, 1152], BF16)
        l1b = lb0[0:16, 0:128]
        bb = lb0[0:16, 128:1152]
        icol = sb("icol", [128, 1], F32)
        RB = sb("RB", [128, 2], F32)

        ps32 = es.enter_context(nc.psum_tensor("ps32", [128, 3584], F32))
        psbf = es.enter_context(nc.psum_tensor("psbf", [128, 1024], BF16))

        sem_names = ["ld", "pe", "ac", "dv", "wt0", "wt1", "wt2", "wt3", "wt4", "wt5", "wt6", "wt7", "out", "ssw", "sw0", "sw1", "sw2", "dbg"]
        sems = {n: es.enter_context(nc.semaphore(n)) for n in sem_names}
        block = es.enter_context(nc.Block())

        p = P(nc, sems)

        _last_p[0] = p

        S1 = sm[:, 0:1]; SQ = sm[:, 1:2]; MEAN = sm[:, 2:3]; MSQ = sm[:, 3:4]
        M2 = sm[:, 4:5]; VAR = sm[:, 5:6]; STD = sm[:, 6:7]; RSTD = sm[:, 7:8]
        T0 = sm[:, 8:9]; NEGB = sm[:, 9:10]; Zc = sm[:, 10:11]; RZ = sm[:, 11:12]
        GMAX = sm[:, 12:13]; SCL = sm[:, 13:14]; RSCL = sm[:, 14:15]
        RED16 = sm[:, 16:32]; QRS = sm[:, 32:48]
        RED4 = sm[:, 48:52]; KRS = sm[:, 52:56]
        T2 = sm[:, 15:16]
        ONEP = sm[:, 56:60]; NMR = sm[:, 60:64]

        def sched():
            ssw_war = {0: 0, 1: 0, 2: 0}
            p0_state = {}

            p.op("sync", lambda e: e.dma_start(out=tri[:], in_=tri_d), incs=["ld"], dma=True)
            p.op("sync", lambda e: e.dma_start(out=j4b, in_=j4_d), incs=["ld"], dma=True)
            p.op("sync", lambda e: e.dma_start(out=sl4, in_=sl4_d), incs=["ld"], dma=True)
            p.op("sync", lambda e: e.dma_start(out=identb[:], in_=identb_d), incs=["ld"], dma=True)
            p.op("sync", lambda e: e.dma_start(out=icol[:], in_=icol_d), incs=["ld"], dma=True)
            for c in range(8):
                p.op("sync", lambda e, c=c: e.dma_start(out=abt[:, c:c + 1], in_=abias_d[c * 128:(c + 1) * 128].rearrange("(p one) -> p one", one=1)), incs=["ld"], dma=True)
                p.op("sync", lambda e, c=c: e.dma_start(out=nwt[:, c:c + 1], in_=nw_d[c * 128:(c + 1) * 128].rearrange("(p one) -> p one", one=1)), incs=["ld"], dma=True)
            p.op("vector", lambda e: e.memset(onescb, 1.0), incs=["dv"])
            p.op("vector", lambda e: e.memset(onescf[:], 1.0), incs=["dv"])
            p.op("vector", lambda e: e.memset(carry[:], 0.0), incs=["dv"])
            p.op("vector", lambda e: e.memset(cb[:, 0:1], 1e-5), incs=["dv"])
            p.op("vector", lambda e: e.memset(cb[:, 1:2], 1e-8), incs=["dv"])

            def rowload(dst_bf, src_ap, n):
                p.op("sync", lambda e: e.dma_start(out=rowf[:, 0:n], in_=src_ap.rearrange("(one n) -> one n", one=1)), waits=["dv"], incs=["ld"], dma=True)
                p.op("vector", lambda e: e.tensor_copy(dst_bf[:, 0:n], rowf[:, 0:n]), waits=["ld"], incs=["dv"])

            rowload(bqr, bq_d, D)
            rowload(bkvr, bkv_d, 512)
            rowload(bor, bo_d, D)
            rowload(obr, obias_d, D)

            for hr in range(16):
                p.op("sync", lambda e, hr=hr: e.dma_start(out=rowf2[:, hr * 64:(hr + 1) * 64], in_=qn_d.rearrange("(one n) -> one n", one=1)), waits=(["dv"] if hr == 0 else []), incs=["ld"], dma=True)
            p.op("vector", lambda e: e.tensor_scalar(out=rowf2, in0=rowf2, scalar1=0.125, scalar2=None, op0=AL.mult), waits=["ld"], incs=["dv"])
            p.op("vector", lambda e: e.tensor_copy(qn8r, rowf2[:, :]), waits=["dv"], incs=["dv"])
            p.op("tensor", lambda e: e.matmul(ps32[:, 0:64], onescb, qn8r[:, 0:64], start=True, stop=True), waits=["dv"], incs=["pe"])
            p.op("scalar", lambda e: e.copy(QN8B[:, :], ps32[:, 0:64]), waits=["pe"], incs=["ac"])
            for rr in range(4):
                p.op("sync", lambda e, rr=rr: e.dma_start(out=rowf2[:, rr * 64:(rr + 1) * 64], in_=kn_d.rearrange("(one n) -> one n", one=1)), waits=["ac"], incs=["ld"], dma=True)
            p.op("vector", lambda e: e.tensor_copy(knr, rowf2[:, 0:256]), waits=["ld"], incs=["dv"])
            p.op("tensor", lambda e: e.matmul(ps32[:, 1024:1088], onescb, knr[:, 0:64], start=True, stop=True), waits=["dv"], incs=["pe"])
            p.op("scalar", lambda e: e.copy(KNB[:, :], ps32[:, 1024:1088]), waits=["pe"], incs=["ac"])

            def wstream(dst, dst_off, src_ap, cols, slab_i):
                sl = slab_i % 2
                stg = s_sb[:, sl * 1024: sl * 1024 + cols]
                p.op("sync", lambda e, stg=stg, src_ap=src_ap: e.dma_start(out=stg, in_=src_ap), waits=["dv"], incs=["ld"], dma=True)
                p.op("vector", lambda e, stg=stg, dst=dst, dst_off=dst_off, cols=cols: e.tensor_copy(dst[:, dst_off:dst_off + cols], stg), waits=["ld"], incs=["dv"])

            si = 0
            for c in range(8):
                wstream(wqb, c * D, wq_d[c * 128:(c + 1) * 128, :], D, si); si += 1
            for c in range(8):
                wstream(wkvb, c * 512, wkv_d[c * 128:(c + 1) * 128, :], 512, si); si += 1
            for c in range(8):
                wstream(wob, c * D, wo_d[c * 128:(c + 1) * 128, :], D, si); si += 1
            for c in range(8):
                wstream(ab, c * 16, fa_d[c * 128:(c + 1) * 128, :], 16, si); si += 1

            for c in range(8):
                stg = s_sb[:, (c % 2) * 1024:(c % 2) * 1024 + D]
                p.op("sync", lambda e, stg=stg, c=c: e.dma_start(out=stg, in_=fk_d[c * 128:(c + 1) * 128, :]), waits=["dv"], incs=["ld"], dma=True)
                p.op("vector", lambda e, stg=stg, c=c: e.tensor_reduce(out=T8[:, c:c + 1], in_=stg, axis=AX.X, op=AL.max), waits=["ld"], incs=["dv"])
                p.op("vector", lambda e, stg=stg, c=c: e.tensor_reduce(out=T8[:, 8 + c:9 + c], in_=stg, axis=AX.X, op=AL.min), waits=["dv"], incs=["dv"])
            p.op("vector", lambda e: e.tensor_reduce(out=GMAX, in_=T8[:, 0:8], axis=AX.X, op=AL.max), waits=["dv"], incs=["dv"])
            p.op("vector", lambda e: e.tensor_reduce(out=sm[:, 15:16], in_=T8[:, 8:16], axis=AX.X, op=AL.min), waits=["dv"], incs=["dv"])
            p.op("vector", lambda e: e.tensor_scalar(out=sm[:, 15:16], in0=sm[:, 15:16], scalar1=-1.0, scalar2=None, op0=AL.mult), waits=["dv"], incs=["dv"])
            p.op("vector", lambda e: e.tensor_max(GMAX, GMAX, sm[:, 15:16]), waits=["dv"], incs=["dv"])
            p.op("vector", lambda e: e.memset(onescf, 1.0), waits=["dv"], incs=["dv"])
            p.op("sync", lambda e: e.dma_start(out=gscr, in_=GMAX), waits=["dv"], incs=["ld"], dma=True)
            p.op("sync", lambda e: e.dma_start(out=rowf[0:1, 0:128], in_=gscr.rearrange("p o -> o p")), waits=["ld"], incs=["ld"], dma=True)
            p.op("vector", lambda e: e.tensor_reduce(out=sm[0:1, 13:14], in_=rowf[0:1, 0:128], axis=AX.X, op=AL.max), waits=["ld"], incs=["dv"])
            p.op("vector", lambda e: e.tensor_copy(rowf[0:1, 0:1], sm[0:1, 13:14]), waits=["dv"], incs=["dv"])
            p.op("tensor", lambda e: e.matmul(ps32[:, 1664:1665], onescf, rowf[0:1, 0:1], start=True, stop=True), waits=["dv"], incs=["pe"])
            p.op("scalar", lambda e: e.copy(GMAX, ps32[:, 1664:1665]), waits=["pe"], incs=["ac"])
            p.op("vector", lambda e: e.tensor_scalar(out=SCL, in0=GMAX, scalar1=1.0 / 7.0, scalar2=None, op0=AL.mult), waits=["ac"], incs=["dv"])
            p.op("vector", lambda e: e.reciprocal(RSCL, SCL), waits=["dv"], incs=["dv"])
            for c in range(8):
                stg = s_sb[:, (c % 2) * 1024:(c % 2) * 1024 + D]
                p.op("sync", lambda e, stg=stg, c=c: e.dma_start(out=stg, in_=fk_d[c * 128:(c + 1) * 128, :]), waits=["dv"], incs=["ld"], dma=True)
                p.op("vector", lambda e, stg=stg: e.tensor_scalar(out=stg, in0=stg, scalar1=RSCL, scalar2=None, op0=AL.mult), waits=["ld"], incs=["dv"])
                p.op("vector", lambda e, stg=stg: e.tensor_scalar(out=stg, in0=stg, scalar1=12582912.0, scalar2=-12582912.0, op0=AL.add, op1=AL.add), waits=["dv"], incs=["dv"])
                p.op("vector", lambda e, stg=stg: e.tensor_scalar(out=stg, in0=stg, scalar1=-8.0, scalar2=7.0, op0=AL.max, op1=AL.min), waits=["dv"], incs=["dv"])
                p.op("vector", lambda e, stg=stg, c=c: e.tensor_copy(ffnq[:, c * D:(c + 1) * D], stg), waits=["dv"], incs=["dv"])
            if dbg:
                p.op("gpsimd", lambda e: e.dma_start(out=dbg_t["dffnq"], in_=ffnq[:, 0:D]), waits=["dv"], incs=["dbg"], dma=True)
                p.op("gpsimd", lambda e: e.dma_start(out=dbg_t["dscl"], in_=sm[:, 13:15]), waits=["dv"], incs=["dbg"], dma=True)
                p.op("vector", lambda e: e.engine_nop(), waits=["dbg"], incs=["dv"])
            p.op("sync", lambda e: e.dma_start(out=rowf[:, 0:D], in_=fb_d.rearrange("(one n) -> one n", one=1)), waits=["dv"], incs=["ld"], dma=True)
            p.op("vector", lambda e: e.tensor_scalar(out=rowf[:, 0:D], in0=rowf[:, 0:D], scalar1=RSCL[0:1, 0:1], scalar2=None, op0=AL.mult), waits=["ld"], incs=["dv"])
            p.op("vector", lambda e: e.tensor_copy(fbr[:, 0:D], rowf[:, 0:D]), waits=["dv"], incs=["dv"])
            p.op("sync", lambda e: e.dma_start(out=fbm32, in_=fbm_d), waits=["dv"], incs=["ld"], dma=True)
            p.op("vector", lambda e: e.tensor_scalar(out=fbm32, in0=fbm32, scalar1=RSCL[0:16, 0:1], scalar2=None, op0=AL.mult), waits=["ld"], incs=["dv"])
            p.op("vector", lambda e: e.tensor_copy(bb, fbm32), waits=["dv"], incs=["dv"])

            for wi, wd in enumerate([aw_d, bw_d, cw_d, dw_d]):
                for ncx in range(8):
                    sl = ncx % 2
                    stg = s_sb[:, sl * 1024: sl * 1024 + D]
                    stgb = Wb[:, sl * 1024: sl * 1024 + D]
                    p.op("sync", lambda e, stg=stg, wd=wd, ncx=ncx: e.dma_start(out=stg, in_=wd[ncx * 128:(ncx + 1) * 128, :]), waits=["dv", "ssw"], incs=["ld"], dma=True)
                    p.op("vector", lambda e, stg=stg, stgb=stgb: e.tensor_copy(stgb, stg), waits=["ld", "pe"], incs=["dv"])
                    for dc in range(8):
                        p.op("tensor", lambda e, stgb=stgb, dc=dc: e.transpose(psbf[:, dc * 128:(dc + 1) * 128], stgb[:, dc * 128:(dc + 1) * 128], identb[:, :]), waits=["dv", "ac"], incs=(["pe"] if dc == 7 else []))
                    for dc in range(8):
                        p.op("vector", lambda e, dc=dc: e.tensor_scalar(out=Wn[:, dc * 128:(dc + 1) * 128], in0=psbf[:, dc * 128:(dc + 1) * 128], scalar1=nwt[:, dc:dc + 1], scalar2=None, op0=AL.mult), waits=["pe"], incs=(["dv"] if dc == 7 else []))
                    p.op("sync", lambda e, wi=wi, ncx=ncx: e.dma_start(
                        out=wt_scr[wi][:, ncx * 128:(ncx + 1) * 128].rearrange("(dc p) c -> p dc c", p=128),
                        in_=Wn[:, 0:1024].rearrange("p (dc c) -> p dc c", c=128)), waits=["dv"], incs=["ssw"], dma=True)
            p0_state["ssw"] = p.cnt["ssw"]
            for dc in range(8):
                sl = dc % 2
                stg = s_sb[:, sl * 1024: sl * 1024 + D]
                stgb = Wb[:, sl * 1024: sl * 1024 + D]
                p.op("sync", lambda e, stg=stg, dc=dc: e.dma_start(out=stg, in_=ow_d[dc * 128:(dc + 1) * 128, :]), waits=["dv", "ssw"], incs=["ld"], dma=True)
                p.op("vector", lambda e, stg=stg, stgb=stgb: e.tensor_copy(stgb, stg), waits=["ld", "pe"], incs=["dv"])
                for ncb in range(8):
                    p.op("tensor", lambda e, stgb=stgb, ncb=ncb: e.transpose(psbf[:, ncb * 128:(ncb + 1) * 128], stgb[:, ncb * 128:(ncb + 1) * 128], identb[:, :]), waits=["dv", "ac"], incs=(["pe"] if ncb == 7 else []))
                for ncb in range(8):
                    p.op("scalar", lambda e, ncb=ncb, dc=dc: e.copy(outwT[:, ncb * 1024 + dc * 128: ncb * 1024 + (dc + 1) * 128], psbf[:, ncb * 128:(ncb + 1) * 128]), waits=["pe"], incs=(["ac"] if ncb == 7 else []))

            def layer_norm(src_ap, ldx=None, dvl=None):
                p.op("vector", lambda e: e.reduce_sum(out=S1, in_=src_ap, axis=AX.X), waits=(["dv", "ac"] if ldx is None else [("ld", ldx)]), incs=["dv"])
                p.op("scalar", lambda e: e.activation(out=ps32[:, 2560:3584], in_=src_ap, func=AF.Square, accum_out=SQ), waits=(["ac", "dv"] if ldx is None else [("ld", ldx), ("dv", dvl)]), incs=["ac"])
                p.op("vector", lambda e: e.tensor_scalar(out=MEAN, in0=S1, scalar1=1.0 / D, scalar2=None, op0=AL.mult), waits=["dv"], incs=["dv"])
                p.op("vector", lambda e: e.tensor_scalar(out=MSQ, in0=SQ, scalar1=1.0 / D, scalar2=None, op0=AL.mult), waits=["ac", "dv"], incs=["dv"])
                p.op("vector", lambda e: e.tensor_mul(M2, MEAN, MEAN), waits=["dv"], incs=["dv"])
                p.op("vector", lambda e: e.tensor_sub(VAR, MSQ, M2), waits=["dv"], incs=["dv"])
                p.op("scalar", lambda e: e.activation(out=STD, in_=VAR, func=AF.Sqrt, bias=cb[:, 0:1], scale=1.0), waits=["dv"], incs=["ac"])
                p.op("vector", lambda e: e.reciprocal(RSTD, STD), waits=["ac"], incs=["dv"])
                p.op("vector", lambda e: e.tensor_mul(T0, MEAN, RSTD), waits=["dv"], incs=["dv"])
                p.op("vector", lambda e: e.tensor_scalar(out=NEGB, in0=T0, scalar1=-1.0, scalar2=None, op0=AL.mult), waits=["dv"], incs=["dv"])
                p.op("scalar", lambda e: e.activation(out=xn_bf[:, :], in_=src_ap, func=AF.Identity, bias=NEGB, scale=RSTD), waits=["dv"], incs=["ac"])

            def transpose_to(dst, src, nblk, dst_stride=128, w0=None):
                for c in range(nblk):
                    p.op("tensor", lambda e, c=c: e.transpose(psbf[:, c * 128:(c + 1) * 128], src[:, c * 128:(c + 1) * 128], identb[:, :]), waits=((["ac", "dv"] if w0 is None else w0) if c == 0 else []), incs=(["pe"] if c == nblk - 1 else []))
                for c in range(nblk):
                    p.op("scalar", lambda e, c=c: e.copy(dst[:, c * dst_stride:c * dst_stride + 128], psbf[:, c * 128:(c + 1) * 128]), waits=["pe"], incs=(["ac"] if c == nblk - 1 else []))

            def ffn(rslot, tap=False, defer_lora=False, dv_ae=None):
                layer_norm(rslot)
                if tap:
                    p.op("gpsimd", lambda e: e.dma_start(out=dbg_t["dxn2"], in_=xn_bf[:, :]), waits=["ac"], incs=["dbg"], dma=True)
                    p.op("vector", lambda e: e.engine_nop(), waits=["dbg"], incs=["dv"])
                transpose_to(xnT, xn_bf, 8, w0=["ac"])
                if ph < 52:
                    return
                for c in range(8):
                    w0 = (["ac", "dv"] if dv_ae is None else ["ac", ("dv", dv_ae)]) if c == 0 else []
                    p.op("tensor", lambda e, c=c: e.matmul(ps32[0:16, 2560:2688], ab[:, c * 16:(c + 1) * 16], xnT[:, c * 128:(c + 1) * 128], start=(c == 0), stop=(c == 7)), waits=w0, incs=(["pe"] if c == 7 else []))
                p.op("scalar", lambda e: e.copy(l1b, ps32[0:16, 2560:2688]), waits=["pe"], incs=["ac"])
                ac_l1 = p.cnt["ac"]
                for c in range(8):
                    w0 = [("ac", ac_l1)] if c == 0 else []
                    p.op("tensor", lambda e, c=c: e.matmul(ps32[:, 2560:3072], xnT[:, c * 128:(c + 1) * 128], ffnq[:, c * D:c * D + 512], start=(c == 0), stop=False), waits=w0)
                    p.op("tensor", lambda e, c=c: e.matmul(ps32[:, 3072:3584], xnT[:, c * 128:(c + 1) * 128], ffnq[:, c * D + 512:(c + 1) * D], start=(c == 0), stop=False))
                p.op("tensor", lambda e: e.matmul(ps32[:, 2560:3072], onescb, fbr[:, 0:512], start=False, stop=True))
                p.op("tensor", lambda e: e.matmul(ps32[:, 3072:3584], onescb, fbr[:, 512:1024], start=False, stop=True), incs=["pe"])
                p.op("vector", lambda e, rslot=rslot: e.scalar_tensor_tensor(out=rslot, in0=ps32[:, 2560:3584], scalar=SCL, in1=rslot, op0=AL.mult, op1=AL.add), waits=["pe", "dv"], incs=["dv"])
                dv_y = p.cnt["dv"]

                def lora(dv_y=dv_y, rslot=rslot):
                    p.op("tensor", lambda e: e.matmul(ps32[:, 2560:3072], l1b, bb[:, 0:512], start=True, stop=True), waits=[("dv", dv_y)])
                    p.op("tensor", lambda e: e.matmul(ps32[:, 3072:3584], l1b, bb[:, 512:1024], start=True, stop=True), incs=["pe"])
                    p.op("vector", lambda e, rslot=rslot: e.scalar_tensor_tensor(out=rslot, in0=ps32[:, 2560:3584], scalar=SCL, in1=rslot, op0=AL.mult, op1=AL.add), waits=["pe", "dv"], incs=["dv"])
                if defer_lora:
                    return lora
                lora()

            def prep(t, ldx=None, dvl=None):
                rslot = res[:, (t % 4) * D:(t % 4 + 1) * D]
                if ldx is None:
                    p.op("sync", lambda e, t=t, rslot=rslot: e.dma_start(out=rslot, in_=x_d[t * 128:(t + 1) * 128, :]), waits=["out", "dv"], incs=["ld"], dma=True)
                    ldx = p.cnt["ld"]
                    dvl = p.cnt["dv"]

                layer_norm(rslot, ldx=ldx, dvl=dvl)
                transpose_to(xnT, xn_bf, 8, w0=["ac"])

                for c in range(8):
                    w0 = ["ac", ("dv", dvl)] if c == 0 else []
                    p.op("tensor", lambda e, c=c: e.matmul(ps32[:, 2560:3072], xnT[:, c * 128:(c + 1) * 128], wqb[:, c * D:c * D + 512], start=(c == 0), stop=False), waits=w0)
                    p.op("tensor", lambda e, c=c: e.matmul(ps32[:, 3072:3584], xnT[:, c * 128:(c + 1) * 128], wqb[:, c * D + 512:(c + 1) * D], start=(c == 0), stop=False))
                p.op("tensor", lambda e: e.matmul(ps32[:, 2560:3072], onescb, bqr[:, 0:512], start=False, stop=True))
                p.op("tensor", lambda e: e.matmul(ps32[:, 3072:3584], onescb, bqr[:, 512:1024], start=False, stop=True), incs=["pe"])
                p.op("scalar", lambda e: e.copy(q16[:, :], ps32[:, 2560:3584]), waits=["pe"], incs=["ac"])
                ac_q = p.cnt["ac"]
                for c in range(8):
                    w0 = [("ac", ac_q)] if c == 0 else []
                    p.op("tensor", lambda e, c=c: e.matmul(ps32[:, 2560:3072], xnT[:, c * 128:(c + 1) * 128], wkvb[:, c * 512:(c + 1) * 512], start=(c == 0), stop=False), waits=w0)
                p.op("tensor", lambda e: e.matmul(ps32[:, 2560:3072], onescb, bkvr[:, 0:512], start=False, stop=True), incs=["pe"])
                p.op("scalar", lambda e: e.copy(kv16[:, :], ps32[:, 2560:3072]), waits=["pe"], incs=["ac"])
                if dbg and t == 0:
                    p.op("gpsimd", lambda e: e.dma_start(out=dbg_t["dq"], in_=q16[:, :]), waits=["ac"], incs=["dbg"], dma=True)
                    p.op("gpsimd", lambda e: e.dma_start(out=dbg_t["dkv"], in_=kv16[:, :]), waits=["ac"], incs=["dbg"], dma=True)
                    p.op("vector", lambda e: e.engine_nop(), waits=["dbg"], incs=["dv"])

                p.op("scalar", lambda e: e.activation(out=ps32[:, 2560:3584], in_=q16[:, :], func=AF.Square), waits=["ac"], incs=["ac"])
                p.op("vector", lambda e: e.tensor_reduce(out=RED16, in_=ps32[:, 2560:3584].rearrange("p (h d) -> p h d", d=64), axis=AX.X, op=AL.add), waits=["ac"], incs=["dv"])
                p.op("scalar", lambda e: e.activation(out=RED16, in_=RED16, func=AF.Sqrt, bias=cb[:, 1:2], scale=1.0 / 64.0), waits=["dv"], incs=["ac"])
                p.op("vector", lambda e: e.reciprocal(QRS, RED16), waits=["ac"], incs=["dv"])
                p.op("vector", lambda e: e.tensor_tensor(out=q16[:, :].rearrange("p (h d) -> p h d", d=64), in0=q16[:, :].rearrange("p (h d) -> p h d", d=64), in1=QRS.to_broadcast([128, 16, 64]), op=AL.mult), waits=["dv"], incs=["dv"])
                p.op("vector", lambda e: e.tensor_tensor(out=xn_bf[:, :].rearrange("p (h d) -> p h d", d=64), in0=q16[:, :].rearrange("p (h d) -> p h d", d=64), in1=QN8B[:, :].rearrange("p (x d) -> p x d", x=1).to_broadcast([128, 16, 64]), op=AL.mult), waits=["dv", "pe"], incs=["dv"])
                p.op("scalar", lambda e: e.activation(out=ps32[:, 2560:2816], in_=kv16[:, 0:256], func=AF.Square), waits=["dv"], incs=["ac"])
                p.op("vector", lambda e: e.tensor_reduce(out=RED4, in_=ps32[:, 2560:2816].rearrange("p (h d) -> p h d", d=64), axis=AX.X, op=AL.add), waits=["ac"], incs=["dv"])
                p.op("scalar", lambda e: e.activation(out=RED4, in_=RED4, func=AF.Sqrt, bias=cb[:, 1:2], scale=1.0 / 64.0), waits=["dv"], incs=["ac"])
                p.op("vector", lambda e: e.reciprocal(KRS, RED4), waits=["ac"], incs=["dv"])
                p.op("vector", lambda e: e.tensor_tensor(out=kv16[:, 0:256].rearrange("p (h d) -> p h d", d=64), in0=kv16[:, 0:256].rearrange("p (h d) -> p h d", d=64), in1=KRS.to_broadcast([128, 4, 64]), op=AL.mult), waits=["dv"], incs=["dv"])
                p.op("vector", lambda e: e.tensor_tensor(out=kv16[:, 0:256].rearrange("p (h d) -> p h d", d=64), in0=kv16[:, 0:256].rearrange("p (h d) -> p h d", d=64), in1=KNB[:, :].rearrange("p (x d) -> p x d", x=1).to_broadcast([128, 4, 64]), op=AL.mult), waits=["dv"], incs=["dv"])
                if dbg and t == 0:
                    p.op("gpsimd", lambda e: e.dma_start(out=dbg_t["dqh"], in_=xn_bf[:, :]), waits=["dv"], incs=["dbg"], dma=True)
                    p.op("gpsimd", lambda e: e.dma_start(out=dbg_t["dkh"], in_=kv16[:, 0:256]), waits=["dv"], incs=["dbg"], dma=True)
                    p.op("vector", lambda e: e.engine_nop(), waits=["dbg"], incs=["dv"])
                p.op("scalar", lambda e, t=t: e.copy(V[:, t * 256:(t + 1) * 256], kv16[:, 256:512]), waits=["ac"], incs=["ac"])
                dv_qk = p.cnt["dv"]

                for c in range(8):
                    p.op("tensor", lambda e, c=c: e.transpose(psbf[:, c * 128:(c + 1) * 128], xn_bf[:, c * 128:(c + 1) * 128], identb[:, :]), waits=(["ac", ("dv", dv_qk)] if c == 0 else []), incs=(["pe"] if c == 7 else []))
                qT = qTb[t % 2]
                for h2 in range(H):
                    srch = (h2 % 2) * 64
                    dsth = ((h2 // 4) % 2) * 64
                    bq_ = (h2 % 4) + 4 * (h2 // 8)
                    p.op("scalar", lambda e, qT=qT, h2=h2, srch=srch, dsth=dsth, bq_=bq_: e.copy(qT[dsth:dsth + 64, bq_ * 128:(bq_ + 1) * 128], psbf[srch:srch + 64, (h2 // 2) * 128:(h2 // 2) * 128 + 128]), waits=["pe"], incs=(["ac"] if h2 == H - 1 else []))
                for b2 in range(2):
                    p.op("tensor", lambda e, b2=b2: e.transpose(psbf[:, b2 * 128:(b2 + 1) * 128], kv16[:, b2 * 128:(b2 + 1) * 128], identb[:, :]), waits=(["ac", ("dv", dv_qk)] if b2 == 0 else []), incs=(["pe"] if b2 == 1 else []))
                for b2 in range(2):
                    for half in range(2):
                        p.op("scalar", lambda e, b2=b2, half=half, t=t: e.copy(kt[b2][half * 64:(half + 1) * 64, t * 128:(t + 1) * 128], psbf[half * 64:(half + 1) * 64, b2 * 128:(b2 + 1) * 128]), waits=["pe"], incs=(["ac"] if (b2 == 1 and half == 1) else []))

            prep(0)
            pend = {"tail": None, "ffnb": None, "ldx": None, "dvl": None}
            hs = {"ac_exp": None, "ac_sc": None, "dv_mask": 0, "pending_av": None}
            for t in range(nt):
                slot = t % 4
                L = 128 * (t + 1)
                rslot = res[:, slot * D:(slot + 1) * D]

                nkc = L // 128
                qT = qTb[t % 2]
                hs["pending_av"] = None

                def stage_mm(h):
                    g2 = h // 4
                    ktile = kt[g2 // 2]
                    kbase = (g2 % 2) * 64
                    qslice = qT[((h // 4) % 2) * 64:((h // 4) % 2) * 64 + 64, ((h % 4) + 4 * (h // 8)) * 128:((h % 4) + 4 * (h // 8)) * 128 + 128]
                    slope_h = float((2.0 ** (-(np.arange(H, dtype=np.float64) / (H - 1)) * 8.0))[h])
                    p.op("vector", lambda e, t=t, slope_h=slope_h, h=h: e.tensor_scalar(out=RB[:, h % 2:h % 2 + 1], in0=icol[:, 0:1], scalar1=float(t * 128), scalar2=-slope_h, op0=AL.add, op1=AL.mult), waits=["ac"], incs=["dv"])
                    off1 = max(0, L - pass_cap)
                    wins = [(off1, L - off1)] + ([(0, off1)] if off1 > 0 else [])
                    pcs = []
                    off, W = wins[0]
                    first = True
                    n0 = 0
                    while n0 < W:
                        nn = min(512, W - n0)
                        w0 = ((["ac"] if hs["ac_exp"] is None else [("ac", hs["ac_exp"])]) if first else [])
                        first = False
                        p.op("tensor", lambda e, n0=n0, nn=nn, off=off, ktile=ktile, kbase=kbase, qslice=qslice: e.matmul(ps32[:, n0:n0 + nn], qslice, ktile[kbase:kbase + 64, off + n0:off + n0 + nn], start=True, stop=False), waits=w0)
                        n0 += nn
                    n0 = 0
                    while n0 < W:
                        nn = min(512, W - n0)
                        last = (n0 + nn >= W)
                        p.op("tensor", lambda e, n0=n0, nn=nn, off=off, h=h: e.matmul(ps32[:, n0:n0 + nn], sl4[:, h * 128:(h + 1) * 128], j4b[:, off + n0:off + n0 + nn], start=False, stop=True), incs=(["pe"] if last else []))
                        n0 += nn
                    pe_sc0 = p.cnt["pe"]
                    dloc = (L - 128) - off
                    p.op("vector", lambda e, dloc=dloc: e.tensor_tensor(out=ps32[:, dloc:dloc + 128], in0=ps32[:, dloc:dloc + 128], in1=tri[:, :], op=AL.add), waits=[("pe", pe_sc0)], incs=["dv"])
                    dv_diag = p.cnt["dv"]
                    return wins, pe_sc0, qslice, ktile, kbase, dv_diag

                def stage_cp(h, st):
                    wins, pe_sc0, qslice, ktile, kbase, dv_diag = st
                    dv_m = max(hs["dv_mask"], dv_diag)
                    off, W = wins[0]
                    p.op("scalar", lambda e, off=off, W=W, h=h: e.activation(out=s_sb[:, off:off + W], in_=ps32[:, 0:W], func=AF.Identity, bias=RB[:, h % 2:h % 2 + 1], scale=1.0), waits=[("pe", pe_sc0), ("dv", dv_m)], incs=["ac"])
                    sc0 = p.cnt["ac"]
                    p.op("scalar", lambda e, off=off, W=W, h=h: e.activation(out=Wb[:, off:off + W], in_=ps32[:, 0:W], func=AF.Exp, bias=RB[:, h % 2:h % 2 + 1], scale=1.0), incs=["ac"])
                    sc1 = p.cnt["ac"]
                    if len(wins) > 1:
                        off, W = wins[1]
                        first = True
                        n0 = 0
                        while n0 < W:
                            nn = min(512, W - n0)
                            w0 = [("ac", sc1)] if first else []
                            first = False
                            p.op("tensor", lambda e, n0=n0, nn=nn, off=off, ktile=ktile, kbase=kbase, qslice=qslice: e.matmul(ps32[:, n0:n0 + nn], qslice, ktile[kbase:kbase + 64, off + n0:off + n0 + nn], start=True, stop=False), waits=w0)
                            n0 += nn
                        n0 = 0
                        while n0 < W:
                            nn = min(512, W - n0)
                            last = (n0 + nn >= W)
                            p.op("tensor", lambda e, n0=n0, nn=nn, off=off, h=h: e.matmul(ps32[:, n0:n0 + nn], sl4[:, h * 128:(h + 1) * 128], j4b[:, off + n0:off + n0 + nn], start=False, stop=True), incs=(["pe"] if last else []))
                            n0 += nn
                        pe_sc1 = p.cnt["pe"]
                        p.op("scalar", lambda e, off=off, W=W, h=h: e.activation(out=s_sb[:, off:off + W], in_=ps32[:, 0:W], func=AF.Identity, bias=RB[:, h % 2:h % 2 + 1], scale=1.0), waits=[("pe", pe_sc1)], incs=["ac"])
                        hs["ac_sc"] = p.cnt["ac"]
                        p.op("scalar", lambda e, off=off, W=W, h=h: e.activation(out=Wb[:, off:off + W], in_=ps32[:, 0:W], func=AF.Exp, bias=RB[:, h % 2:h % 2 + 1], scale=1.0), incs=["ac"])
                    else:
                        hs["ac_sc"] = sc0
                    hs["ac_exp"] = p.cnt["ac"]

                if ph >= 2:
                    stage_cp(0, stage_mm(0))
                for h in (range(H) if ph >= 2 else []):
                    g = h // 4
                    for r in range(8):
                        p.op("vector", lambda e, r=r, L=L: e.max(out=T8[:, 8 * r:8 * r + 8], in_=s_sb[:, 0:L]), waits=(([("ac", hs["ac_sc"])] if hs["ac_sc"] else ["ac"]) + ["dv"] if r == 0 else ["dv"]), incs=["dv"])
                        p.op("vector", lambda e, r=r, L=L: e.match_replace(out=s_sb[:, 0:L], in_to_replace=T8[:, 8 * r:8 * r + 8], in_values=s_sb[:, 0:L], imm_value=RIMM), waits=["dv"], incs=["dv"])
                    st_next = stage_mm(h + 1) if h + 1 < H else None
                    p.op("vector", lambda e, L=L: e.scalar_tensor_tensor(out=Wn[:, 0:L], in0=s_sb[:, 0:L], scalar=FLAGT, in1=Wb[:, 0:L], op0=AL.is_le, op1=AL.mult, accum_out=Zc), waits=["dv", ("ac", hs["ac_exp"]), "wt0", "wt1", "wt2", "wt3", "wt4", "wt5", "wt6", "wt7"], incs=["dv"])
                    hs["dv_mask"] = p.cnt["dv"]
                    p.op("vector", lambda e: e.reciprocal(RZ, Zc), waits=["dv"], incs=["dv"])
                    p.op("vector", lambda e, L=L: e.tensor_scalar(out=Wn[:, 0:L], in0=Wn[:, 0:L], scalar1=RZ, scalar2=None, op0=AL.mult), waits=["dv"], incs=["dv"])
                    if st_next is not None:
                        stage_cp(h + 1, st_next)
                    if h == 1 and pend["tail"] is not None:
                        pend["ffnb"] = pend["tail"]()
                        pend["tail"] = None
                    if hs.get("pending_av") is not None:
                        hs["pending_av"]()
                        hs["pending_av"] = None
                    if h == 2:
                        if pend["ffnb"] is not None:
                            pend["ffnb"]()
                            pend["ffnb"] = None
                        if slot != 3 and t + 1 < nt:
                            p.op("sync", lambda e, t=t: e.dma_start(out=res[:, ((t + 1) % 4) * D:((t + 1) % 4 + 1) * D], in_=x_d[(t + 1) * 128:(t + 2) * 128, :]), waits=["out"], incs=["ld"], dma=True)
                            pend["ldx"] = p.cnt["ld"]
                            pend["dvl"] = p.cnt["dv"]
                    if h == 8 and slot != 3 and t + 1 < nt:
                        prep(t + 1, ldx=pend["ldx"], dvl=pend["dvl"])
                    if dbg and t == 0 and h == 0:
                        p.op("gpsimd", lambda e: e.dma_start(out=dbg_t["dT8"], in_=T8[:, :]), waits=["dv"], incs=["dbg"], dma=True)
                        p.op("gpsimd", lambda e: e.dma_start(out=dbg_t["dwn"], in_=Wn[:, 0:128]), waits=["dv"], incs=["dbg"], dma=True)
                        p.op("vector", lambda e: e.engine_nop(), waits=["dbg"], incs=["dv"])
                    wch = "wt%d" % (h % 8)
                    if ph < 3:
                        continue
                    for kc in range(nkc):
                        p.op("sync", lambda e, kc=kc: e.dma_start_transpose(WT[:, kc * 128:(kc + 1) * 128], Wn[:, kc * 128:(kc + 1) * 128]), waits=["dv", "pe"], incs=[wch], dma=True)

                    def av_emit(h=h, g=g, nkc=nkc, wch=wch):
                        for kc in range(nkc):
                            w0 = [wch, "ac"] if kc == 0 else []
                            p.op("tensor", lambda e, kc=kc, g=g, nkc=nkc: e.matmul(ps32[0:64, 2048:2176], V[:, kc * 256 + g * 64: kc * 256 + (g + 1) * 64], WT[:, kc * 128:(kc + 1) * 128], start=(kc == 0), stop=(kc == nkc - 1)), waits=w0, incs=(["pe"] if kc == nkc - 1 else []))
                        p.op("scalar", lambda e, h=h: e.copy(CTXT[(h % 2) * 64:(h % 2) * 64 + 64, (h // 2) * 128:(h // 2) * 128 + 128], ps32[0:64, 2048:2176]), waits=["pe"], incs=["ac"])
                    hs["pending_av"] = av_emit

                if hs.get("pending_av") is not None:
                    hs["pending_av"]()
                    hs["pending_av"] = None

                def make_tail(t=t, rslot=rslot):
                    ac_t = p.cnt["ac"]
                    dv_t = p.cnt["dv"]
                    def tail():
                        for c in (range(8) if ph >= 4 else []):
                            w0 = [("ac", ac_t), ("dv", dv_t)] if c == 0 else []
                            p.op("tensor", lambda e, c=c: e.matmul(ps32[:, 2560:3072], CTXT[:, c * 128:(c + 1) * 128], wob[:, c * D:c * D + 512], start=(c == 0), stop=False), waits=w0)
                            p.op("tensor", lambda e, c=c: e.matmul(ps32[:, 3072:3584], CTXT[:, c * 128:(c + 1) * 128], wob[:, c * D + 512:(c + 1) * D], start=(c == 0), stop=False))
                        fb = None
                        if ph >= 4:
                            p.op("tensor", lambda e: e.matmul(ps32[:, 2560:3072], onescb, bor[:, 0:512], start=False, stop=True))
                            p.op("tensor", lambda e: e.matmul(ps32[:, 3072:3584], onescb, bor[:, 512:1024], start=False, stop=True), incs=["pe"])
                            if dbg and t == 0:
                                p.op("gpsimd", lambda e: e.dma_start(out=dbg_t["dctx"], in_=CTXT[:, :]), waits=["ac"], incs=["dbg"], dma=True)
                                p.op("vector", lambda e: e.engine_nop(), waits=["dbg"], incs=["dv"])
                            p.op("vector", lambda e, rslot=rslot: e.scalar_tensor_tensor(out=rslot, in0=ps32[:, 2560:3584], scalar=1.0, in1=rslot, op0=AL.mult, op1=AL.add), waits=["pe", "dv"], incs=["dv"])
                            if dbg and t == 0:
                                p.op("gpsimd", lambda e: e.dma_start(out=dbg_t["dres1"], in_=rslot), waits=["dv"], incs=["dbg"], dma=True)
                                p.op("vector", lambda e: e.engine_nop(), waits=["dbg"], incs=["dv"])
                        dv_ae = p.cnt["dv"]
                        if ph >= 5:
                            fb = ffn(rslot, tap=(dbg and t == 0), defer_lora=True, dv_ae=dv_ae)
                        if dbg and t == 0:
                            p.op("gpsimd", lambda e: e.dma_start(out=dbg_t["dres2"], in_=rslot), waits=["dv"], incs=["dbg"], dma=True)
                            p.op("vector", lambda e: e.engine_nop(), waits=["dbg"], incs=["dv"])
                        return fb
                    return tail

                if slot == 3 or ph < 2:
                    fb = make_tail()()
                    if fb is not None:
                        fb()
                    if ph < 2 and slot != 3 and t + 1 < nt:
                        prep(t + 1)
                else:
                    pend["tail"] = make_tail()

                if slot == 3:
                    G = t // 4
                    for tt in (range(4) if ph >= 6 else []):
                        rs = res[:, tt * D:(tt + 1) * D]
                        p.op("vector", lambda e, rs=rs: e.reduce_sum(out=S1, in_=rs, axis=AX.X), waits=["dv", "ac"], incs=["dv"])
                        p.op("scalar", lambda e, rs=rs: e.activation(out=ps32[:, 2560:3584], in_=rs, func=AF.Square, accum_out=SQ), waits=["ac", "dv"], incs=["ac"])
                        p.op("vector", lambda e: e.tensor_scalar(out=MEAN, in0=S1, scalar1=1.0 / D, scalar2=None, op0=AL.mult), waits=["dv"], incs=["dv"])
                        p.op("vector", lambda e: e.tensor_scalar(out=MSQ, in0=SQ, scalar1=1.0 / D, scalar2=None, op0=AL.mult), waits=["ac", "dv"], incs=["dv"])
                        p.op("vector", lambda e: e.tensor_mul(M2, MEAN, MEAN), waits=["dv"], incs=["dv"])
                        p.op("vector", lambda e: e.tensor_sub(VAR, MSQ, M2), waits=["dv"], incs=["dv"])
                        p.op("scalar", lambda e: e.activation(out=STD, in_=VAR, func=AF.Sqrt, bias=cb[:, 0:1], scale=1.0), waits=["dv"], incs=["ac"])
                        p.op("vector", lambda e: e.reciprocal(RSTD, STD), waits=["ac"], incs=["dv"])
                        p.op("vector", lambda e: e.tensor_mul(T2, VAR, RSTD), waits=["dv"], incs=["dv"])
                        p.op("vector", lambda e: e.tensor_mul(T2, T2, RSTD), waits=["dv"], incs=["dv"])
                        p.op("scalar", lambda e: e.activation(out=T0, in_=T2, func=AF.Sqrt, bias=cb[:, 0:1], scale=1.0), waits=["dv"], incs=["ac"])
                        p.op("vector", lambda e: e.reciprocal(T2, T0), waits=["ac"], incs=["dv"])
                        p.op("vector", lambda e, tt=tt: e.tensor_scalar(out=ONEP[:, tt:tt + 1], in0=RSTD, scalar1=1.0, scalar2=None, op0=AL.add), waits=["dv"], incs=["dv"])
                        p.op("vector", lambda e, tt=tt: e.tensor_mul(NMR[:, tt:tt + 1], MEAN, RSTD), waits=["dv"], incs=["dv"])
                        p.op("vector", lambda e, tt=tt: e.tensor_scalar(out=NMR[:, tt:tt + 1], in0=NMR[:, tt:tt + 1], scalar1=-1.0, scalar2=None, op0=AL.mult), waits=["dv"], incs=["dv"])
                        p.op("vector", lambda e: e.tensor_mul(T0, RSTD, T2), waits=["dv"], incs=["dv"])
                        p.op("vector", lambda e: e.tensor_mul(NEGB, MEAN, T0), waits=["dv"], incs=["dv"])
                        p.op("vector", lambda e: e.tensor_scalar(out=NEGB, in0=NEGB, scalar1=-1.0, scalar2=None, op0=AL.mult), waits=["dv"], incs=["dv"])
                        p.op("scalar", lambda e, rs=rs: e.activation(out=xn_bf[:, :], in_=rs, func=AF.Identity, bias=NEGB, scale=T0), waits=["dv"], incs=["ac"])
                        for c in range(8):
                            p.op("tensor", lambda e, c=c: e.transpose(psbf[:, c * 128:(c + 1) * 128], xn_bf[:, c * 128:(c + 1) * 128], identb[:, :]), waits=["ac", "dv"], incs=(["pe"] if c == 7 else []))
                        for c in range(8):
                            p.op("scalar", lambda e, c=c, tt=tt: e.copy(xn3T[:, c * 512 + tt * 128: c * 512 + (tt + 1) * 128], psbf[:, c * 128:(c + 1) * 128]), waits=["pe"], incs=(["ac"] if c == 7 else []))
                    for ncx in (range(8) if ph >= 6 else []):
                        b0 = ((2 * ncx) % 3) * 2 * D
                        b1 = ((2 * ncx + 1) % 3) * 2 * D
                        for ph01, wis in ((0, (0, 3)), (1, (1, 2))):
                            slot = (2 * ncx + ph01) % 3
                            base = slot * 2 * D
                            for k2, wi in enumerate(wis):
                                p.op("sync", lambda e, wi=wi, ncx=ncx, base=base, k2=k2, slot=slot: e.dma_start(
                                    out=ssw[:, base + k2 * D: base + (k2 + 1) * D].rearrange("p (dc c) -> p dc c", c=128),
                                    in_=wt_scr[wi][:, ncx * 128:(ncx + 1) * 128].rearrange("(dc p) c -> p dc c", p=128)), waits=[("pe", ssw_war[slot]), ("ssw", p0_state["ssw"])], incs=["sw%d" % slot], dma=True)
                        for k2 in range(2):
                            for dc in range(8):
                                w0 = ["sw%d" % ((2 * ncx) % 3), "ac", "dv"] if (k2 == 0 and dc == 0) else []
                                reg = 2560 if k2 == 0 else 3072
                                p.op("tensor", lambda e, k2=k2, dc=dc, b0=b0, reg=reg: e.matmul(ps32[:, reg:reg + 512], ssw[:, b0 + k2 * D + dc * 128: b0 + k2 * D + (dc + 1) * 128], xn3T[:, dc * 512:(dc + 1) * 512], start=(dc == 0), stop=(dc == 7)), waits=w0, incs=(["pe"] if (k2 == 1 and dc == 7) else []))
                        ssw_war[(2 * ncx) % 3] = p.cnt["pe"]
                        p.op("scalar", lambda e, ncx=ncx: e.activation(out=asg[:, :], in_=ps32[:, 2560:3072], func=AF.Sigmoid, bias=abt[:, ncx:ncx + 1], scale=1.0), waits=["pe"], incs=["ac"])
                        p.op("scalar", lambda e: e.activation(out=dsg[:, :], in_=ps32[:, 3072:3584], func=AF.Sigmoid), incs=["ac"])
                        for k2 in range(2):
                            for dc in range(8):
                                w0 = ["sw%d" % ((2 * ncx + 1) % 3), ("ac", p.cnt["ac"])] if (k2 == 0 and dc == 0) else []
                                reg = 2560 if k2 == 0 else 3072
                                p.op("tensor", lambda e, k2=k2, dc=dc, b1=b1, reg=reg: e.matmul(ps32[:, reg:reg + 512], ssw[:, b1 + k2 * D + dc * 128: b1 + k2 * D + (dc + 1) * 128], xn3T[:, dc * 512:(dc + 1) * 512], start=(dc == 0), stop=(dc == 7)), waits=w0, incs=(["pe"] if (k2 == 1 and dc == 7) else []))
                        ssw_war[(2 * ncx + 1) % 3] = p.cnt["pe"]
                        p.op("scalar", lambda e: e.activation(out=csg[:, :], in_=ps32[:, 3072:3584], func=AF.Sigmoid), waits=["pe"], incs=["ac"])
                        p.op("vector", lambda e: e.tensor_mul(dbt[:, :], ps32[:, 2560:3072], dsg[:, :]), waits=["ac", "dv"], incs=["dv"])
                        p.op("vector", lambda e, ncx=ncx: e.tensor_tensor_scan(out=hh[:, :], data0=asg[:, :], data1=dbt[:, :], initial=carry[:, ncx:ncx + 1], op0=AL.mult, op1=AL.add), waits=["dv"], incs=["dv"])
                        p.op("vector", lambda e, ncx=ncx: e.tensor_copy(carry[:, ncx:ncx + 1], hh[:, 511:512]), waits=["dv"], incs=["dv"])
                        p.op("vector", lambda e, ncx=ncx: e.tensor_mul(Hp[:, ncx * 512:(ncx + 1) * 512], csg[:, :], hh[:, :]), waits=["dv"], incs=["dv"])
                    for tt in range(4):
                        rs = res[:, tt * D:(tt + 1) * D]
                        for ncx in (range(8) if ph >= 6 else []):
                            w0 = ["dv", "ac"] if ncx == 0 else []
                            p.op("tensor", lambda e, ncx=ncx, tt=tt: e.matmul(ps32[:, 2560:3072], Hp[:, ncx * 512 + tt * 128: ncx * 512 + (tt + 1) * 128], outwT[:, ncx * 1024: ncx * 1024 + 512], start=(ncx == 0), stop=False), waits=w0)
                            p.op("tensor", lambda e, ncx=ncx, tt=tt: e.matmul(ps32[:, 3072:3584], Hp[:, ncx * 512 + tt * 128: ncx * 512 + (tt + 1) * 128], outwT[:, ncx * 1024 + 512:(ncx + 1) * 1024], start=(ncx == 0), stop=False))
                        if ph >= 6:
                            p.op("tensor", lambda e: e.matmul(ps32[:, 2560:3072], onescb, obr[:, 0:512], start=False, stop=True))
                            p.op("tensor", lambda e: e.matmul(ps32[:, 3072:3584], onescb, obr[:, 512:1024], start=False, stop=True), incs=["pe"])
                            p.op("vector", lambda e, rs=rs, tt=tt: e.tensor_scalar(out=rs, in0=rs, scalar1=ONEP[:, tt:tt + 1], scalar2=NMR[:, tt:tt + 1], op0=AL.mult, op1=AL.add), waits=["pe", "dv"], incs=["dv"])
                            p.op("vector", lambda e, rs=rs: e.scalar_tensor_tensor(out=rs, in0=ps32[:, 2560:3584], scalar=1.0, in1=rs, op0=AL.mult, op1=AL.add), waits=["dv"], incs=["dv"])
                        if dbg and G == 0 and tt == 0:
                            p.op("gpsimd", lambda e, rs=rs: e.dma_start(out=dbg_t["dres3"], in_=rs), waits=["dv"], incs=["dbg"], dma=True)
                            p.op("vector", lambda e: e.engine_nop(), waits=["dbg"], incs=["dv"])
                        if ph >= 7:
                            ffn(rs)
                        p.op("sync", lambda e, G=G, tt=tt, rs=rs: e.dma_start(out=y_d[(G * 4 + tt) * 128:(G * 4 + tt + 1) * 128, :], in_=rs), waits=["dv"], incs=["out"], dma=True)
                    if t + 1 < nt:
                        prep(t + 1)

        for name, deco in [("sync", block.sync), ("tensor", block.tensor),
                           ("scalar", block.scalar), ("vector", block.vector),
                           ("gpsimd", block.gpsimd)]:
            def body(eng, name=name):
                p.begin(name, eng)
                sched()
                if name == "sync":
                    eng.wait_ge(sems["out"], nt * 16)
            deco(body)

    return nc


_NC_CACHE = None


def kernel(**inputs):
    global _NC_CACHE
    xs = np.ascontiguousarray(np.asarray(inputs["L0_attn_a_x"], dtype=np.float32))
    B = xs.shape[0]
    wmap = {
        "wq": "L0_attn_a_wq", "bq": "L0_attn_a_bq", "wkv": "L0_attn_a_wkv",
        "bkv": "L0_attn_a_bkv", "wo": "L0_attn_a_wo", "bo": "L0_attn_a_bo",
        "qn": "L0_attn_a_qn_w", "kn": "L0_attn_a_kn_w", "nw": "L0_attn_b_norm_w",
        "aw": "L0_attn_b_A_w", "abias": "L0_attn_b_A_b", "bw": "L0_attn_b_B_w",
        "cw": "L0_attn_b_C_w", "dw": "L0_attn_b_D_w", "ow": "L0_attn_b_out_w",
        "obias": "L0_attn_b_out_b", "fk": "L0_ffn_kernel", "fb": "L0_ffn_bias",
        "fa": "L0_ffn_a", "fbm": "L0_ffn_b",
    }
    shared = {k: np.ascontiguousarray(np.asarray(inputs[v], dtype=np.float32))
              for k, v in wmap.items()}
    if _NC_CACHE is None:
        _NC_CACHE = build_nc()
    nc = _NC_CACHE
    in_maps = [dict(shared, x=np.ascontiguousarray(xs[b])) for b in range(B)]
    r = run_bass_kernel_spmd(nc, in_maps, core_ids=list(range(B)))
    out = np.stack([np.asarray(r.results[b]["y"], dtype=np.float32) for b in range(B)])
    return out


# version 16
# speedup vs baseline: 1.2003x; 1.2003x over previous
import numpy as np
import ml_dtypes
import concourse.bass as bass
import concourse.mybir as mybir
from concourse.bass_utils import run_bass_kernel_spmd
from contextlib import ExitStack

F32 = mybir.dt.float32
BF16 = mybir.dt.bfloat16
I32 = mybir.dt.int32
AL = mybir.AluOpType
AF = mybir.ActivationFunctionType
AX = mybir.AxisListType

T = 4096
D = 1024
H = 16
NT = 32
RIMM = -2.0e30
FLAGT = -1.5e30
NEG = -1.0e30


def _consts():
    tri = np.zeros((128, 128), dtype=np.float32)
    tri[np.triu_indices(128, 1)] = NEG
    j = np.arange(T, dtype=np.float64)
    j_hi = np.floor(j / 32.0) * 32.0
    j_lo = j - j_hi
    j4 = np.stack([j_hi, j_lo, j_hi, j_lo]).astype(np.float32)
    slopes = (2.0 ** (-(np.arange(H, dtype=np.float64) / (H - 1)) * 8.0)).astype(np.float32)
    s_hi = slopes.astype(ml_dtypes.bfloat16).astype(np.float32)
    s_lo = (slopes - s_hi).astype(np.float32)
    sl4 = np.zeros((4, H * 128), dtype=np.float32)
    for h in range(H):
        sl4[0, h * 128:(h + 1) * 128] = s_hi[h]
        sl4[1, h * 128:(h + 1) * 128] = s_hi[h]
        sl4[2, h * 128:(h + 1) * 128] = s_lo[h]
        sl4[3, h * 128:(h + 1) * 128] = s_lo[h]
    icol = np.arange(128, dtype=np.float32).reshape(128, 1)
    return (tri, j4.astype(ml_dtypes.bfloat16), sl4.astype(ml_dtypes.bfloat16),
            np.eye(128, dtype=ml_dtypes.bfloat16), icol)


class P:

    def __init__(self, nc, sems):
        self.nc = nc
        self.sems = sems
        self.cnt = None
        self.eng_name = None
        self.eng = None
        self.last_wait = None
        self.tags = {}
        self.eng_pc = 0

    def begin(self, eng_name, eng):
        self.eng_name = eng_name
        self.eng = eng
        self.cnt = {k: 0 for k in self.sems}
        self.last_wait = {k: 0 for k in self.sems}
        self.tags[eng_name] = []
        self.eng_pc = 0

    def op(self, eng, fn, waits=(), incs=(), dma=False):
        if eng == self.eng_name:
            import sys as _sys
            _ln = _sys._getframe(1).f_lineno
            _i0 = self.nc.n_instructions()
            for w in waits:
                if isinstance(w, tuple):
                    w, v = w
                else:
                    v = self.cnt[w]
                if v > 0 and v > self.last_wait[w]:
                    self.eng.wait_ge(self.sems[w], v)
                    self.last_wait[w] = v
            inst = fn(self.eng)
            _n = self.nc.n_instructions() - _i0
            self.tags[self.eng_name].append((self.eng_pc, self.eng_pc + _n, _ln))
            self.eng_pc += _n
            amt = 16 if dma else 1
            if incs:
                assert inst is not None
                for s in incs:
                    inst.then_inc(self.sems[s], amt)
        amt = 16 if dma else 1
        for s in incs:
            self.cnt[s] += amt


_last_p = [None]


def build_nc(nt=NT, dbg=False, ph=99, pass_cap=2048):
    Tl = 128 * nt
    nc = bass.Bass("TRN2", target_bir_lowering=False)

    x_d = nc.dram_tensor("x", [Tl, D], F32, kind="ExternalInput").ap()
    y_d = nc.dram_tensor("y", [Tl, D], F32, kind="ExternalOutput").ap()
    wq_d = nc.dram_tensor("wq", [D, D], F32, kind="ExternalInput").ap()
    bq_d = nc.dram_tensor("bq", [D], F32, kind="ExternalInput").ap()
    wkv_d = nc.dram_tensor("wkv", [D, 512], F32, kind="ExternalInput").ap()
    bkv_d = nc.dram_tensor("bkv", [512], F32, kind="ExternalInput").ap()
    wo_d = nc.dram_tensor("wo", [D, D], F32, kind="ExternalInput").ap()
    bo_d = nc.dram_tensor("bo", [D], F32, kind="ExternalInput").ap()
    qn_d = nc.dram_tensor("qn", [64], F32, kind="ExternalInput").ap()
    kn_d = nc.dram_tensor("kn", [64], F32, kind="ExternalInput").ap()
    nw_d = nc.dram_tensor("nw", [D], F32, kind="ExternalInput").ap()
    aw_d = nc.dram_tensor("aw", [D, D], F32, kind="ExternalInput").ap()
    abias_d = nc.dram_tensor("abias", [D], F32, kind="ExternalInput").ap()
    bw_d = nc.dram_tensor("bw", [D, D], F32, kind="ExternalInput").ap()
    cw_d = nc.dram_tensor("cw", [D, D], F32, kind="ExternalInput").ap()
    dw_d = nc.dram_tensor("dw", [D, D], F32, kind="ExternalInput").ap()
    ow_d = nc.dram_tensor("ow", [D, D], F32, kind="ExternalInput").ap()
    obias_d = nc.dram_tensor("obias", [D], F32, kind="ExternalInput").ap()
    fk_d = nc.dram_tensor("fk", [D, D], F32, kind="ExternalInput").ap()
    fb_d = nc.dram_tensor("fb", [D], F32, kind="ExternalInput").ap()
    fa_d = nc.dram_tensor("fa", [D, 16], F32, kind="ExternalInput").ap()
    fbm_d = nc.dram_tensor("fbm", [16, D], F32, kind="ExternalInput").ap()

    tri_c, j4_c, sl4_c, identb_c, icol_c = _consts()
    icol_d = nc.inline_tensor(icol_c, "icol_c").ap()
    tri_d = nc.inline_tensor(tri_c, "tri_c").ap()
    j4_d = nc.inline_tensor(j4_c, "j4_c").ap()
    sl4_d = nc.inline_tensor(sl4_c, "sl4_c").ap()
    identb_d = nc.inline_tensor(identb_c, "identb_c").ap()

    wt_scr = [nc.dram_tensor(f"wtscr{i}", [D, D], BF16, kind="Internal").ap()
              for i in range(4)]
    gscr = nc.dram_tensor("gscr", [128, 1], F32, kind="Internal").ap()
    dbg_t = {}
    if dbg:
        for nm, shp in [("dq", [128, D]), ("dkv", [128, 512]), ("dqh", [128, D]),
                        ("dkh", [128, 256]), ("ds", [128, 128]), ("dwb", [128, 128]),
                        ("dT8", [128, 64]), ("dwn", [128, 128]), ("dctx", [128, D]),
                        ("dres1", [128, D]), ("dres2", [128, D]), ("dres3", [128, D]),
                        ("dffnq", [128, D]), ("dxn2", [128, D]), ("dscl", [128, 2])]:
            dbg_t[nm] = nc.dram_tensor(nm, shp, F32, kind="ExternalOutput").ap()

    es = ExitStack()
    sb = lambda n, s, dt: es.enter_context(nc.sbuf_tensor(n, s, dt))
    with es:
        wqb = sb("wqb", [128, 8 * D], BF16)
        wkvb = sb("wkvb", [128, 8 * 512], BF16)
        wob = sb("wob", [128, 8 * D], BF16)
        ffnq = sb("ffnq", [128, 8 * D], BF16)
        ab = sb("ab", [128, 8 * 16], BF16)
        outwT = sb("outwT", [128, 8 * D], BF16)
        ssw = sb("ssw", [128, 6 * D], BF16)
        kt = [sb("kt0", [128, Tl], BF16), sb("kt1", [128, Tl], BF16)]
        V = sb("V", [128, nt * 256], BF16)
        res = sb("res", [128, 4 * D], F32)
        s_sb = sb("s_sb", [128, max(Tl, 4096)], F32)
        Wb = sb("Wbuf", [128, max(Tl, 4096)], BF16)
        Wn = sb("Wnorm", [128, max(Tl, 4096)], BF16)
        WT = sb("WT", [128, max(Tl, 4096)], BF16)
        qTb = [sb("qT0", [128, D], BF16), sb("qT1", [128, D], BF16)]
        xnT = sb("xnT", [128, D], BF16)
        xn_bf = sb("xn_bf", [128, D], BF16)
        CTXT = sb("CTXT", [128, D], BF16)
        q16 = sb("q16", [128, D], BF16)
        kv16 = sb("kv16", [128, 512], BF16)
        xn3T = WT
        Hp = Wb
        asg = sb("asg", [128, 512], F32)
        dsg = sb("dsg", [128, 512], F32)
        csg = sb("csg", [128, 512], BF16)
        dbt = dsg
        hh = dbt
        QN8B = sb("QN8B", [128, 64], BF16)
        KNB = sb("KNB", [128, 64], BF16)
        tri = sb("tri", [128, 128], F32)
        identb = sb("identb", [128, 128], BF16)
        sh = sb("sh", [128, 6144], BF16)
        j4b = sh[0:4, 0:T]
        sl4 = sh[0:4, 4096:4096 + H * 128]
        onescb = sh[64:65, 5888:6016]
        bqr = sh[64:65, 0:D]
        bor = sh[64:65, 1024:2048]
        fbr = sh[64:65, 2048:3072]
        obr = sh[64:65, 3072:4096]
        bkvr = sh[64:65, 4096:4608]
        qn8r = sh[64:65, 4608:5632]
        knr = sh[64:65, 5632:5888]
        onescf = s_sb[0:1, 3200:3328]
        rowf = s_sb[0:1, 2048:3072]
        rowf2 = s_sb[0:1, 3072:4096]
        fbm32 = s_sb[0:16, 3072:4096]
        abt = sb("abt", [128, 8], F32)
        nwt = sb("nwt", [128, 8], F32)
        T8 = sb("T8", [128, 64], F32)
        sm = sb("sm", [128, 64], F32)
        carry = sb("carry", [128, 8], F32)
        cb = sb("cb", [128, 2], F32)
        lb0 = sb("lb0", [16, 1152], BF16)
        l1b = lb0[0:16, 0:128]
        bb = lb0[0:16, 128:1152]
        icol = sb("icol", [128, 1], F32)
        RB = sb("RB", [128, 2], F32)

        ps32 = es.enter_context(nc.psum_tensor("ps32", [128, 3584], F32))
        psbf = es.enter_context(nc.psum_tensor("psbf", [128, 1024], BF16))

        sem_names = ["ld", "pe", "ac", "dv", "wt0", "wt1", "wt2", "wt3", "wt4", "wt5", "wt6", "wt7", "out", "ssw", "sw0", "sw1", "sw2", "dbg"]
        sems = {n: es.enter_context(nc.semaphore(n)) for n in sem_names}
        block = es.enter_context(nc.Block())

        p = P(nc, sems)

        _last_p[0] = p

        S1 = sm[:, 0:1]; SQ = sm[:, 1:2]; MEAN = sm[:, 2:3]; MSQ = sm[:, 3:4]
        M2 = sm[:, 4:5]; VAR = sm[:, 5:6]; STD = sm[:, 6:7]; RSTD = sm[:, 7:8]
        T0 = sm[:, 8:9]; NEGB = sm[:, 9:10]; Zc = sm[:, 10:11]; RZ = sm[:, 11:12]
        GMAX = sm[:, 12:13]; SCL = sm[:, 13:14]; RSCL = sm[:, 14:15]
        RED16 = sm[:, 16:32]; QRS = sm[:, 32:48]
        RED4 = sm[:, 48:52]; KRS = sm[:, 52:56]
        T2 = sm[:, 15:16]
        ONEP = sm[:, 56:60]; NMR = sm[:, 60:64]

        def sched():
            ssw_war = {0: 0, 1: 0, 2: 0}
            p0_state = {}

            p.op("sync", lambda e: e.dma_start(out=tri[:], in_=tri_d), incs=["ld"], dma=True)
            p.op("sync", lambda e: e.dma_start(out=j4b, in_=j4_d), incs=["ld"], dma=True)
            p.op("sync", lambda e: e.dma_start(out=sl4, in_=sl4_d), incs=["ld"], dma=True)
            p.op("sync", lambda e: e.dma_start(out=identb[:], in_=identb_d), incs=["ld"], dma=True)
            p.op("sync", lambda e: e.dma_start(out=icol[:], in_=icol_d), incs=["ld"], dma=True)
            for c in range(8):
                p.op("sync", lambda e, c=c: e.dma_start(out=abt[:, c:c + 1], in_=abias_d[c * 128:(c + 1) * 128].rearrange("(p one) -> p one", one=1)), incs=["ld"], dma=True)
                p.op("sync", lambda e, c=c: e.dma_start(out=nwt[:, c:c + 1], in_=nw_d[c * 128:(c + 1) * 128].rearrange("(p one) -> p one", one=1)), incs=["ld"], dma=True)
            p.op("vector", lambda e: e.memset(onescb, 1.0), incs=["dv"])
            p.op("vector", lambda e: e.memset(onescf[:], 1.0), incs=["dv"])
            p.op("vector", lambda e: e.memset(carry[:], 0.0), incs=["dv"])
            p.op("vector", lambda e: e.memset(cb[:, 0:1], 1e-5), incs=["dv"])
            p.op("vector", lambda e: e.memset(cb[:, 1:2], 1e-8), incs=["dv"])

            def rowload(dst_bf, src_ap, n):
                p.op("sync", lambda e: e.dma_start(out=rowf[:, 0:n], in_=src_ap.rearrange("(one n) -> one n", one=1)), waits=["dv"], incs=["ld"], dma=True)
                p.op("vector", lambda e: e.tensor_copy(dst_bf[:, 0:n], rowf[:, 0:n]), waits=["ld"], incs=["dv"])

            rowload(bqr, bq_d, D)
            rowload(bkvr, bkv_d, 512)
            rowload(bor, bo_d, D)
            rowload(obr, obias_d, D)

            for hr in range(16):
                p.op("sync", lambda e, hr=hr: e.dma_start(out=rowf2[:, hr * 64:(hr + 1) * 64], in_=qn_d.rearrange("(one n) -> one n", one=1)), waits=(["dv"] if hr == 0 else []), incs=["ld"], dma=True)
            p.op("vector", lambda e: e.tensor_scalar(out=rowf2, in0=rowf2, scalar1=0.125, scalar2=None, op0=AL.mult), waits=["ld"], incs=["dv"])
            p.op("vector", lambda e: e.tensor_copy(qn8r, rowf2[:, :]), waits=["dv"], incs=["dv"])
            p.op("tensor", lambda e: e.matmul(ps32[:, 0:64], onescb, qn8r[:, 0:64], start=True, stop=True), waits=["dv"], incs=["pe"])
            p.op("scalar", lambda e: e.copy(QN8B[:, :], ps32[:, 0:64]), waits=["pe"], incs=["ac"])
            for rr in range(4):
                p.op("sync", lambda e, rr=rr: e.dma_start(out=rowf2[:, rr * 64:(rr + 1) * 64], in_=kn_d.rearrange("(one n) -> one n", one=1)), waits=["ac"], incs=["ld"], dma=True)
            p.op("vector", lambda e: e.tensor_copy(knr, rowf2[:, 0:256]), waits=["ld"], incs=["dv"])
            p.op("tensor", lambda e: e.matmul(ps32[:, 1024:1088], onescb, knr[:, 0:64], start=True, stop=True), waits=["dv"], incs=["pe"])
            p.op("scalar", lambda e: e.copy(KNB[:, :], ps32[:, 1024:1088]), waits=["pe"], incs=["ac"])

            def wstream(dst, dst_off, src_ap, cols, slab_i):
                sl = slab_i % 2
                stg = s_sb[:, sl * 1024: sl * 1024 + cols]
                p.op("sync", lambda e, stg=stg, src_ap=src_ap: e.dma_start(out=stg, in_=src_ap), waits=["dv"], incs=["ld"], dma=True)
                p.op("vector", lambda e, stg=stg, dst=dst, dst_off=dst_off, cols=cols: e.tensor_copy(dst[:, dst_off:dst_off + cols], stg), waits=["ld"], incs=["dv"])

            si = 0
            for c in range(8):
                wstream(wqb, c * D, wq_d[c * 128:(c + 1) * 128, :], D, si); si += 1
            for c in range(8):
                wstream(wkvb, c * 512, wkv_d[c * 128:(c + 1) * 128, :], 512, si); si += 1
            for c in range(8):
                wstream(wob, c * D, wo_d[c * 128:(c + 1) * 128, :], D, si); si += 1
            for c in range(8):
                wstream(ab, c * 16, fa_d[c * 128:(c + 1) * 128, :], 16, si); si += 1

            for c in range(8):
                stg = s_sb[:, (c % 2) * 1024:(c % 2) * 1024 + D]
                p.op("sync", lambda e, stg=stg, c=c: e.dma_start(out=stg, in_=fk_d[c * 128:(c + 1) * 128, :]), waits=["dv"], incs=["ld"], dma=True)
                p.op("vector", lambda e, stg=stg, c=c: e.tensor_reduce(out=T8[:, c:c + 1], in_=stg, axis=AX.X, op=AL.max), waits=["ld"], incs=["dv"])
                p.op("vector", lambda e, stg=stg, c=c: e.tensor_reduce(out=T8[:, 8 + c:9 + c], in_=stg, axis=AX.X, op=AL.min), waits=["dv"], incs=["dv"])
            p.op("vector", lambda e: e.tensor_reduce(out=GMAX, in_=T8[:, 0:8], axis=AX.X, op=AL.max), waits=["dv"], incs=["dv"])
            p.op("vector", lambda e: e.tensor_reduce(out=sm[:, 15:16], in_=T8[:, 8:16], axis=AX.X, op=AL.min), waits=["dv"], incs=["dv"])
            p.op("vector", lambda e: e.tensor_scalar(out=sm[:, 15:16], in0=sm[:, 15:16], scalar1=-1.0, scalar2=None, op0=AL.mult), waits=["dv"], incs=["dv"])
            p.op("vector", lambda e: e.tensor_max(GMAX, GMAX, sm[:, 15:16]), waits=["dv"], incs=["dv"])
            p.op("vector", lambda e: e.memset(onescf, 1.0), waits=["dv"], incs=["dv"])
            p.op("sync", lambda e: e.dma_start(out=gscr, in_=GMAX), waits=["dv"], incs=["ld"], dma=True)
            p.op("sync", lambda e: e.dma_start(out=rowf[0:1, 0:128], in_=gscr.rearrange("p o -> o p")), waits=["ld"], incs=["ld"], dma=True)
            p.op("vector", lambda e: e.tensor_reduce(out=sm[0:1, 13:14], in_=rowf[0:1, 0:128], axis=AX.X, op=AL.max), waits=["ld"], incs=["dv"])
            p.op("vector", lambda e: e.tensor_copy(rowf[0:1, 0:1], sm[0:1, 13:14]), waits=["dv"], incs=["dv"])
            p.op("tensor", lambda e: e.matmul(ps32[:, 1664:1665], onescf, rowf[0:1, 0:1], start=True, stop=True), waits=["dv"], incs=["pe"])
            p.op("scalar", lambda e: e.copy(GMAX, ps32[:, 1664:1665]), waits=["pe"], incs=["ac"])
            p.op("vector", lambda e: e.tensor_scalar(out=SCL, in0=GMAX, scalar1=1.0 / 7.0, scalar2=None, op0=AL.mult), waits=["ac"], incs=["dv"])
            p.op("vector", lambda e: e.reciprocal(RSCL, SCL), waits=["dv"], incs=["dv"])
            for c in range(8):
                stg = s_sb[:, (c % 2) * 1024:(c % 2) * 1024 + D]
                p.op("sync", lambda e, stg=stg, c=c: e.dma_start(out=stg, in_=fk_d[c * 128:(c + 1) * 128, :]), waits=["dv"], incs=["ld"], dma=True)
                p.op("vector", lambda e, stg=stg: e.tensor_scalar(out=stg, in0=stg, scalar1=RSCL, scalar2=None, op0=AL.mult), waits=["ld"], incs=["dv"])
                p.op("vector", lambda e, stg=stg: e.tensor_scalar(out=stg, in0=stg, scalar1=12582912.0, scalar2=-12582912.0, op0=AL.add, op1=AL.add), waits=["dv"], incs=["dv"])
                p.op("vector", lambda e, stg=stg: e.tensor_scalar(out=stg, in0=stg, scalar1=-8.0, scalar2=7.0, op0=AL.max, op1=AL.min), waits=["dv"], incs=["dv"])
                p.op("vector", lambda e, stg=stg, c=c: e.tensor_copy(ffnq[:, c * D:(c + 1) * D], stg), waits=["dv"], incs=["dv"])
            if dbg:
                p.op("gpsimd", lambda e: e.dma_start(out=dbg_t["dffnq"], in_=ffnq[:, 0:D]), waits=["dv"], incs=["dbg"], dma=True)
                p.op("gpsimd", lambda e: e.dma_start(out=dbg_t["dscl"], in_=sm[:, 13:15]), waits=["dv"], incs=["dbg"], dma=True)
                p.op("vector", lambda e: e.engine_nop(), waits=["dbg"], incs=["dv"])
            p.op("sync", lambda e: e.dma_start(out=rowf[:, 0:D], in_=fb_d.rearrange("(one n) -> one n", one=1)), waits=["dv"], incs=["ld"], dma=True)
            p.op("vector", lambda e: e.tensor_scalar(out=rowf[:, 0:D], in0=rowf[:, 0:D], scalar1=RSCL[0:1, 0:1], scalar2=None, op0=AL.mult), waits=["ld"], incs=["dv"])
            p.op("vector", lambda e: e.tensor_copy(fbr[:, 0:D], rowf[:, 0:D]), waits=["dv"], incs=["dv"])
            p.op("sync", lambda e: e.dma_start(out=fbm32, in_=fbm_d), waits=["dv"], incs=["ld"], dma=True)
            p.op("vector", lambda e: e.tensor_scalar(out=fbm32, in0=fbm32, scalar1=RSCL[0:16, 0:1], scalar2=None, op0=AL.mult), waits=["ld"], incs=["dv"])
            p.op("vector", lambda e: e.tensor_copy(bb, fbm32), waits=["dv"], incs=["dv"])

            for wi, wd in enumerate([aw_d, bw_d, cw_d, dw_d]):
                for ncx in range(8):
                    sl = ncx % 2
                    stg = s_sb[:, sl * 1024: sl * 1024 + D]
                    stgb = Wb[:, sl * 1024: sl * 1024 + D]
                    p.op("sync", lambda e, stg=stg, wd=wd, ncx=ncx: e.dma_start(out=stg, in_=wd[ncx * 128:(ncx + 1) * 128, :]), waits=["dv", "ssw"], incs=["ld"], dma=True)
                    p.op("vector", lambda e, stg=stg, stgb=stgb: e.tensor_copy(stgb, stg), waits=["ld", "pe"], incs=["dv"])
                    for dc in range(8):
                        p.op("tensor", lambda e, stgb=stgb, dc=dc: e.transpose(psbf[:, dc * 128:(dc + 1) * 128], stgb[:, dc * 128:(dc + 1) * 128], identb[:, :]), waits=["dv", "ac"], incs=(["pe"] if dc == 7 else []))
                    for dc in range(8):
                        p.op("vector", lambda e, dc=dc: e.tensor_scalar(out=Wn[:, dc * 128:(dc + 1) * 128], in0=psbf[:, dc * 128:(dc + 1) * 128], scalar1=nwt[:, dc:dc + 1], scalar2=None, op0=AL.mult), waits=["pe"], incs=(["dv"] if dc == 7 else []))
                    p.op("sync", lambda e, wi=wi, ncx=ncx: e.dma_start(
                        out=wt_scr[wi][:, ncx * 128:(ncx + 1) * 128].rearrange("(dc p) c -> p dc c", p=128),
                        in_=Wn[:, 0:1024].rearrange("p (dc c) -> p dc c", c=128)), waits=["dv"], incs=["ssw"], dma=True)
            p0_state["ssw"] = p.cnt["ssw"]
            for dc in range(8):
                sl = dc % 2
                stg = s_sb[:, sl * 1024: sl * 1024 + D]
                stgb = Wb[:, sl * 1024: sl * 1024 + D]
                p.op("sync", lambda e, stg=stg, dc=dc: e.dma_start(out=stg, in_=ow_d[dc * 128:(dc + 1) * 128, :]), waits=["dv", "ssw"], incs=["ld"], dma=True)
                p.op("vector", lambda e, stg=stg, stgb=stgb: e.tensor_copy(stgb, stg), waits=["ld", "pe"], incs=["dv"])
                for ncb in range(8):
                    p.op("tensor", lambda e, stgb=stgb, ncb=ncb: e.transpose(psbf[:, ncb * 128:(ncb + 1) * 128], stgb[:, ncb * 128:(ncb + 1) * 128], identb[:, :]), waits=["dv", "ac"], incs=(["pe"] if ncb == 7 else []))
                for ncb in range(8):
                    p.op("scalar", lambda e, ncb=ncb, dc=dc: e.copy(outwT[:, ncb * 1024 + dc * 128: ncb * 1024 + (dc + 1) * 128], psbf[:, ncb * 128:(ncb + 1) * 128]), waits=["pe"], incs=(["ac"] if ncb == 7 else []))

            def layer_norm(src_ap, ldx=None, dvl=None):
                p.op("vector", lambda e: e.reduce_sum(out=S1, in_=src_ap, axis=AX.X), waits=(["dv", "ac"] if ldx is None else [("ld", ldx)]), incs=["dv"])
                p.op("scalar", lambda e: e.activation(out=ps32[:, 2560:3584], in_=src_ap, func=AF.Square, accum_out=SQ), waits=(["ac", "dv"] if ldx is None else [("ld", ldx), ("dv", dvl)]), incs=["ac"])
                p.op("vector", lambda e: e.tensor_scalar(out=MEAN, in0=S1, scalar1=1.0 / D, scalar2=None, op0=AL.mult), waits=["dv"], incs=["dv"])
                p.op("vector", lambda e: e.tensor_scalar(out=MSQ, in0=SQ, scalar1=1.0 / D, scalar2=None, op0=AL.mult), waits=["ac", "dv"], incs=["dv"])
                p.op("vector", lambda e: e.tensor_mul(M2, MEAN, MEAN), waits=["dv"], incs=["dv"])
                p.op("vector", lambda e: e.tensor_sub(VAR, MSQ, M2), waits=["dv"], incs=["dv"])
                p.op("scalar", lambda e: e.activation(out=STD, in_=VAR, func=AF.Sqrt, bias=cb[:, 0:1], scale=1.0), waits=["dv"], incs=["ac"])
                p.op("vector", lambda e: e.reciprocal(RSTD, STD), waits=["ac"], incs=["dv"])
                p.op("vector", lambda e: e.tensor_mul(T0, MEAN, RSTD), waits=["dv"], incs=["dv"])
                p.op("vector", lambda e: e.tensor_scalar(out=NEGB, in0=T0, scalar1=-1.0, scalar2=None, op0=AL.mult), waits=["dv"], incs=["dv"])
                p.op("scalar", lambda e: e.activation(out=xn_bf[:, :], in_=src_ap, func=AF.Identity, bias=NEGB, scale=RSTD), waits=["dv"], incs=["ac"])

            def transpose_to(dst, src, nblk, dst_stride=128, w0=None):
                for c in range(nblk):
                    p.op("tensor", lambda e, c=c: e.transpose(psbf[:, c * 128:(c + 1) * 128], src[:, c * 128:(c + 1) * 128], identb[:, :]), waits=((["ac", "dv"] if w0 is None else w0) if c == 0 else []), incs=(["pe"] if c == nblk - 1 else []))
                for c in range(nblk):
                    p.op("scalar", lambda e, c=c: e.copy(dst[:, c * dst_stride:c * dst_stride + 128], psbf[:, c * 128:(c + 1) * 128]), waits=["pe"], incs=(["ac"] if c == nblk - 1 else []))

            def ffn(rslot, tap=False, defer_lora=False, dv_ae=None):
                layer_norm(rslot)
                if tap:
                    p.op("gpsimd", lambda e: e.dma_start(out=dbg_t["dxn2"], in_=xn_bf[:, :]), waits=["ac"], incs=["dbg"], dma=True)
                    p.op("vector", lambda e: e.engine_nop(), waits=["dbg"], incs=["dv"])
                transpose_to(xnT, xn_bf, 8, w0=["ac"])
                if ph < 52:
                    return
                for c in range(8):
                    w0 = (["ac", "dv"] if dv_ae is None else ["ac", ("dv", dv_ae)]) if c == 0 else []
                    p.op("tensor", lambda e, c=c: e.matmul(ps32[0:16, 2560:2688], ab[:, c * 16:(c + 1) * 16], xnT[:, c * 128:(c + 1) * 128], start=(c == 0), stop=(c == 7)), waits=w0, incs=(["pe"] if c == 7 else []))
                p.op("scalar", lambda e: e.copy(l1b, ps32[0:16, 2560:2688]), waits=["pe"], incs=["ac"])
                ac_l1 = p.cnt["ac"]
                for c in range(8):
                    w0 = [("ac", ac_l1)] if c == 0 else []
                    p.op("tensor", lambda e, c=c: e.matmul(ps32[:, 2560:3072], xnT[:, c * 128:(c + 1) * 128], ffnq[:, c * D:c * D + 512], start=(c == 0), stop=False), waits=w0)
                    p.op("tensor", lambda e, c=c: e.matmul(ps32[:, 3072:3584], xnT[:, c * 128:(c + 1) * 128], ffnq[:, c * D + 512:(c + 1) * D], start=(c == 0), stop=False))
                p.op("tensor", lambda e: e.matmul(ps32[:, 2560:3072], onescb, fbr[:, 0:512], start=False, stop=True))
                p.op("tensor", lambda e: e.matmul(ps32[:, 3072:3584], onescb, fbr[:, 512:1024], start=False, stop=True), incs=["pe"])
                p.op("vector", lambda e, rslot=rslot: e.scalar_tensor_tensor(out=rslot, in0=ps32[:, 2560:3584], scalar=SCL, in1=rslot, op0=AL.mult, op1=AL.add), waits=["pe", "dv"], incs=["dv"])
                dv_y = p.cnt["dv"]

                def lora(dv_y=dv_y, rslot=rslot):
                    p.op("tensor", lambda e: e.matmul(ps32[:, 2560:3072], l1b, bb[:, 0:512], start=True, stop=True), waits=[("dv", dv_y)])
                    p.op("tensor", lambda e: e.matmul(ps32[:, 3072:3584], l1b, bb[:, 512:1024], start=True, stop=True), incs=["pe"])
                    p.op("vector", lambda e, rslot=rslot: e.scalar_tensor_tensor(out=rslot, in0=ps32[:, 2560:3584], scalar=SCL, in1=rslot, op0=AL.mult, op1=AL.add), waits=["pe", "dv"], incs=["dv"])
                if defer_lora:
                    return lora
                lora()

            def prep(t, ldx=None, dvl=None):
                rslot = res[:, (t % 4) * D:(t % 4 + 1) * D]
                if ldx is None:
                    p.op("sync", lambda e, t=t, rslot=rslot: e.dma_start(out=rslot, in_=x_d[t * 128:(t + 1) * 128, :]), waits=["out", "dv"], incs=["ld"], dma=True)
                    ldx = p.cnt["ld"]
                    dvl = p.cnt["dv"]

                layer_norm(rslot, ldx=ldx, dvl=dvl)
                transpose_to(xnT, xn_bf, 8, w0=["ac"])

                for c in range(8):
                    w0 = ["ac", ("dv", dvl)] if c == 0 else []
                    p.op("tensor", lambda e, c=c: e.matmul(ps32[:, 2560:3072], xnT[:, c * 128:(c + 1) * 128], wqb[:, c * D:c * D + 512], start=(c == 0), stop=False), waits=w0)
                    p.op("tensor", lambda e, c=c: e.matmul(ps32[:, 3072:3584], xnT[:, c * 128:(c + 1) * 128], wqb[:, c * D + 512:(c + 1) * D], start=(c == 0), stop=False))
                p.op("tensor", lambda e: e.matmul(ps32[:, 2560:3072], onescb, bqr[:, 0:512], start=False, stop=True))
                p.op("tensor", lambda e: e.matmul(ps32[:, 3072:3584], onescb, bqr[:, 512:1024], start=False, stop=True), incs=["pe"])
                p.op("scalar", lambda e: e.copy(q16[:, :], ps32[:, 2560:3584]), waits=["pe"], incs=["ac"])
                ac_q = p.cnt["ac"]
                for c in range(8):
                    w0 = [("ac", ac_q)] if c == 0 else []
                    p.op("tensor", lambda e, c=c: e.matmul(ps32[:, 2560:3072], xnT[:, c * 128:(c + 1) * 128], wkvb[:, c * 512:(c + 1) * 512], start=(c == 0), stop=False), waits=w0)
                p.op("tensor", lambda e: e.matmul(ps32[:, 2560:3072], onescb, bkvr[:, 0:512], start=False, stop=True), incs=["pe"])
                p.op("scalar", lambda e: e.copy(kv16[:, :], ps32[:, 2560:3072]), waits=["pe"], incs=["ac"])
                if dbg and t == 0:
                    p.op("gpsimd", lambda e: e.dma_start(out=dbg_t["dq"], in_=q16[:, :]), waits=["ac"], incs=["dbg"], dma=True)
                    p.op("gpsimd", lambda e: e.dma_start(out=dbg_t["dkv"], in_=kv16[:, :]), waits=["ac"], incs=["dbg"], dma=True)
                    p.op("vector", lambda e: e.engine_nop(), waits=["dbg"], incs=["dv"])

                def prep_b(t=t):
                    self_ = None
                    p.op("scalar", lambda e: e.activation(out=ps32[:, 2560:3584], in_=q16[:, :], func=AF.Square), waits=["ac"], incs=["ac"])
                    p.op("vector", lambda e: e.tensor_reduce(out=RED16, in_=ps32[:, 2560:3584].rearrange("p (h d) -> p h d", d=64), axis=AX.X, op=AL.add), waits=["ac"], incs=["dv"])
                    p.op("scalar", lambda e: e.activation(out=RED16, in_=RED16, func=AF.Sqrt, bias=cb[:, 1:2], scale=1.0 / 64.0), waits=["dv"], incs=["ac"])
                    p.op("vector", lambda e: e.reciprocal(QRS, RED16), waits=["ac"], incs=["dv"])
                    p.op("vector", lambda e: e.tensor_tensor(out=q16[:, :].rearrange("p (h d) -> p h d", d=64), in0=q16[:, :].rearrange("p (h d) -> p h d", d=64), in1=QRS.to_broadcast([128, 16, 64]), op=AL.mult), waits=["dv"], incs=["dv"])
                    p.op("vector", lambda e: e.tensor_tensor(out=xn_bf[:, :].rearrange("p (h d) -> p h d", d=64), in0=q16[:, :].rearrange("p (h d) -> p h d", d=64), in1=QN8B[:, :].rearrange("p (x d) -> p x d", x=1).to_broadcast([128, 16, 64]), op=AL.mult), waits=["dv", "pe"], incs=["dv"])
                    p.op("scalar", lambda e: e.activation(out=ps32[:, 2560:2816], in_=kv16[:, 0:256], func=AF.Square), waits=["dv"], incs=["ac"])
                    p.op("vector", lambda e: e.tensor_reduce(out=RED4, in_=ps32[:, 2560:2816].rearrange("p (h d) -> p h d", d=64), axis=AX.X, op=AL.add), waits=["ac"], incs=["dv"])
                    p.op("scalar", lambda e: e.activation(out=RED4, in_=RED4, func=AF.Sqrt, bias=cb[:, 1:2], scale=1.0 / 64.0), waits=["dv"], incs=["ac"])
                    p.op("vector", lambda e: e.reciprocal(KRS, RED4), waits=["ac"], incs=["dv"])
                    p.op("vector", lambda e: e.tensor_tensor(out=kv16[:, 0:256].rearrange("p (h d) -> p h d", d=64), in0=kv16[:, 0:256].rearrange("p (h d) -> p h d", d=64), in1=KRS.to_broadcast([128, 4, 64]), op=AL.mult), waits=["dv"], incs=["dv"])
                    p.op("vector", lambda e: e.tensor_tensor(out=kv16[:, 0:256].rearrange("p (h d) -> p h d", d=64), in0=kv16[:, 0:256].rearrange("p (h d) -> p h d", d=64), in1=KNB[:, :].rearrange("p (x d) -> p x d", x=1).to_broadcast([128, 4, 64]), op=AL.mult), waits=["dv"], incs=["dv"])
                    if dbg and t == 0:
                        p.op("gpsimd", lambda e: e.dma_start(out=dbg_t["dqh"], in_=xn_bf[:, :]), waits=["dv"], incs=["dbg"], dma=True)
                        p.op("gpsimd", lambda e: e.dma_start(out=dbg_t["dkh"], in_=kv16[:, 0:256]), waits=["dv"], incs=["dbg"], dma=True)
                        p.op("vector", lambda e: e.engine_nop(), waits=["dbg"], incs=["dv"])
                    p.op("scalar", lambda e, t=t: e.copy(V[:, t * 256:(t + 1) * 256], kv16[:, 256:512]), waits=["ac"], incs=["ac"])
                    dv_qk = p.cnt["dv"]

                    for c in range(8):
                        p.op("tensor", lambda e, c=c: e.transpose(psbf[:, c * 128:(c + 1) * 128], xn_bf[:, c * 128:(c + 1) * 128], identb[:, :]), waits=(["ac", ("dv", dv_qk)] if c == 0 else []), incs=(["pe"] if c == 7 else []))
                    qT = qTb[t % 2]
                    for h2 in range(H):
                        srch = (h2 % 2) * 64
                        dsth = ((h2 // 4) % 2) * 64
                        bq_ = (h2 % 4) + 4 * (h2 // 8)
                        p.op("scalar", lambda e, qT=qT, h2=h2, srch=srch, dsth=dsth, bq_=bq_: e.copy(qT[dsth:dsth + 64, bq_ * 128:(bq_ + 1) * 128], psbf[srch:srch + 64, (h2 // 2) * 128:(h2 // 2) * 128 + 128]), waits=["pe"], incs=(["ac"] if h2 == H - 1 else []))
                    for b2 in range(2):
                        p.op("tensor", lambda e, b2=b2: e.transpose(psbf[:, b2 * 128:(b2 + 1) * 128], kv16[:, b2 * 128:(b2 + 1) * 128], identb[:, :]), waits=(["ac", ("dv", dv_qk)] if b2 == 0 else []), incs=(["pe"] if b2 == 1 else []))
                    for b2 in range(2):
                        for half in range(2):
                            p.op("scalar", lambda e, b2=b2, half=half, t=t: e.copy(kt[b2][half * 64:(half + 1) * 64, t * 128:(t + 1) * 128], psbf[half * 64:(half + 1) * 64, b2 * 128:(b2 + 1) * 128]), waits=["pe"], incs=(["ac"] if (b2 == 1 and half == 1) else []))
                return prep_b

            prep(0)()
            pend = {"tail": None, "ffnb": None, "ldx": None, "dvl": None, "pb": None}
            hs = {"ac_exp": None, "ac_sc": None, "dv_mask": 0, "pending_av": None}
            for t in range(nt):
                slot = t % 4
                L = 128 * (t + 1)
                rslot = res[:, slot * D:(slot + 1) * D]

                nkc = L // 128
                qT = qTb[t % 2]
                hs["pending_av"] = None

                def stage_mm(h):
                    g2 = h // 4
                    ktile = kt[g2 // 2]
                    kbase = (g2 % 2) * 64
                    qslice = qT[((h // 4) % 2) * 64:((h // 4) % 2) * 64 + 64, ((h % 4) + 4 * (h // 8)) * 128:((h % 4) + 4 * (h // 8)) * 128 + 128]
                    slope_h = float((2.0 ** (-(np.arange(H, dtype=np.float64) / (H - 1)) * 8.0))[h])
                    p.op("vector", lambda e, t=t, slope_h=slope_h, h=h: e.tensor_scalar(out=RB[:, h % 2:h % 2 + 1], in0=icol[:, 0:1], scalar1=float(t * 128), scalar2=-slope_h, op0=AL.add, op1=AL.mult), waits=["ac"], incs=["dv"])
                    off1 = max(0, L - pass_cap)
                    wins = [(off1, L - off1)] + ([(0, off1)] if off1 > 0 else [])
                    pcs = []
                    off, W = wins[0]
                    first = True
                    n0 = 0
                    while n0 < W:
                        nn = min(512, W - n0)
                        w0 = ((["ac"] if hs["ac_exp"] is None else [("ac", hs["ac_exp"])]) if first else [])
                        first = False
                        p.op("tensor", lambda e, n0=n0, nn=nn, off=off, ktile=ktile, kbase=kbase, qslice=qslice: e.matmul(ps32[:, n0:n0 + nn], qslice, ktile[kbase:kbase + 64, off + n0:off + n0 + nn], start=True, stop=False), waits=w0)
                        n0 += nn
                    n0 = 0
                    while n0 < W:
                        nn = min(512, W - n0)
                        last = (n0 + nn >= W)
                        p.op("tensor", lambda e, n0=n0, nn=nn, off=off, h=h: e.matmul(ps32[:, n0:n0 + nn], sl4[:, h * 128:(h + 1) * 128], j4b[:, off + n0:off + n0 + nn], start=False, stop=True), incs=(["pe"] if last else []))
                        n0 += nn
                    pe_sc0 = p.cnt["pe"]
                    dloc = (L - 128) - off
                    p.op("vector", lambda e, dloc=dloc: e.tensor_tensor(out=ps32[:, dloc:dloc + 128], in0=ps32[:, dloc:dloc + 128], in1=tri[:, :], op=AL.add), waits=[("pe", pe_sc0)], incs=["dv"])
                    dv_diag = p.cnt["dv"]
                    return wins, pe_sc0, qslice, ktile, kbase, dv_diag

                def stage_cp(h, st):
                    wins, pe_sc0, qslice, ktile, kbase, dv_diag = st
                    dv_m = max(hs["dv_mask"], dv_diag)
                    off, W = wins[0]
                    p.op("scalar", lambda e, off=off, W=W, h=h: e.activation(out=s_sb[:, off:off + W], in_=ps32[:, 0:W], func=AF.Identity, bias=RB[:, h % 2:h % 2 + 1], scale=1.0), waits=[("pe", pe_sc0), ("dv", dv_m)], incs=["ac"])
                    sc0 = p.cnt["ac"]
                    p.op("scalar", lambda e, off=off, W=W, h=h: e.activation(out=Wb[:, off:off + W], in_=ps32[:, 0:W], func=AF.Exp, bias=RB[:, h % 2:h % 2 + 1], scale=1.0), incs=["ac"])
                    sc1 = p.cnt["ac"]
                    if len(wins) > 1:
                        off, W = wins[1]
                        first = True
                        n0 = 0
                        while n0 < W:
                            nn = min(512, W - n0)
                            w0 = [("ac", sc1)] if first else []
                            first = False
                            p.op("tensor", lambda e, n0=n0, nn=nn, off=off, ktile=ktile, kbase=kbase, qslice=qslice: e.matmul(ps32[:, n0:n0 + nn], qslice, ktile[kbase:kbase + 64, off + n0:off + n0 + nn], start=True, stop=False), waits=w0)
                            n0 += nn
                        n0 = 0
                        while n0 < W:
                            nn = min(512, W - n0)
                            last = (n0 + nn >= W)
                            p.op("tensor", lambda e, n0=n0, nn=nn, off=off, h=h: e.matmul(ps32[:, n0:n0 + nn], sl4[:, h * 128:(h + 1) * 128], j4b[:, off + n0:off + n0 + nn], start=False, stop=True), incs=(["pe"] if last else []))
                            n0 += nn
                        pe_sc1 = p.cnt["pe"]
                        p.op("scalar", lambda e, off=off, W=W, h=h: e.activation(out=s_sb[:, off:off + W], in_=ps32[:, 0:W], func=AF.Identity, bias=RB[:, h % 2:h % 2 + 1], scale=1.0), waits=[("pe", pe_sc1)], incs=["ac"])
                        hs["ac_sc"] = p.cnt["ac"]
                        p.op("scalar", lambda e, off=off, W=W, h=h: e.activation(out=Wb[:, off:off + W], in_=ps32[:, 0:W], func=AF.Exp, bias=RB[:, h % 2:h % 2 + 1], scale=1.0), incs=["ac"])
                    else:
                        hs["ac_sc"] = sc0
                    hs["ac_exp"] = p.cnt["ac"]

                if ph >= 2:
                    stage_cp(0, stage_mm(0))
                for h in (range(H) if ph >= 2 else []):
                    g = h // 4
                    for r in range(8):
                        p.op("vector", lambda e, r=r, L=L: e.max(out=T8[:, 8 * r:8 * r + 8], in_=s_sb[:, 0:L]), waits=(([("ac", hs["ac_sc"])] if hs["ac_sc"] else ["ac"]) + ["dv"] if r == 0 else ["dv"]), incs=["dv"])
                        p.op("vector", lambda e, r=r, L=L: e.match_replace(out=s_sb[:, 0:L], in_to_replace=T8[:, 8 * r:8 * r + 8], in_values=s_sb[:, 0:L], imm_value=RIMM), waits=["dv"], incs=["dv"])
                    st_next = stage_mm(h + 1) if h + 1 < H else None
                    p.op("vector", lambda e, L=L: e.scalar_tensor_tensor(out=Wn[:, 0:L], in0=s_sb[:, 0:L], scalar=FLAGT, in1=Wb[:, 0:L], op0=AL.is_le, op1=AL.mult, accum_out=Zc), waits=["dv", ("ac", hs["ac_exp"]), "wt%d" % ((h - 1) % 8)], incs=["dv"])
                    hs["dv_mask"] = p.cnt["dv"]
                    p.op("vector", lambda e: e.reciprocal(RZ, Zc), waits=["dv"], incs=["dv"])
                    p.op("vector", lambda e, L=L: e.tensor_scalar(out=Wn[:, 0:L], in0=Wn[:, 0:L], scalar1=RZ, scalar2=None, op0=AL.mult), waits=["dv"], incs=["dv"])
                    if st_next is not None:
                        stage_cp(h + 1, st_next)
                    if h == 1 and pend["tail"] is not None:
                        pend["ffnb"] = pend["tail"]()
                        pend["tail"] = None
                    if hs.get("pending_av") is not None:
                        hs["pending_av"]()
                        hs["pending_av"] = None
                    if h == 2:
                        if pend["ffnb"] is not None:
                            pend["ffnb"]()
                            pend["ffnb"] = None
                        if slot != 3 and t + 1 < nt:
                            p.op("sync", lambda e, t=t: e.dma_start(out=res[:, ((t + 1) % 4) * D:((t + 1) % 4 + 1) * D], in_=x_d[(t + 1) * 128:(t + 2) * 128, :]), waits=["out"], incs=["ld"], dma=True)
                            pend["ldx"] = p.cnt["ld"]
                            pend["dvl"] = p.cnt["dv"]
                    if h == 8 and slot != 3 and t + 1 < nt:
                        pend["pb"] = prep(t + 1, ldx=pend["ldx"], dvl=pend["dvl"])
                    if h == 9 and pend.get("pb") is not None:
                        pend["pb"]()
                        pend["pb"] = None
                    if dbg and t == 0 and h == 0:
                        p.op("gpsimd", lambda e: e.dma_start(out=dbg_t["dT8"], in_=T8[:, :]), waits=["dv"], incs=["dbg"], dma=True)
                        p.op("gpsimd", lambda e: e.dma_start(out=dbg_t["dwn"], in_=Wn[:, 0:128]), waits=["dv"], incs=["dbg"], dma=True)
                        p.op("vector", lambda e: e.engine_nop(), waits=["dbg"], incs=["dv"])
                    wch = "wt%d" % (h % 8)
                    if ph < 3:
                        continue
                    for kc in range(nkc):
                        p.op("sync", lambda e, kc=kc: e.dma_start_transpose(WT[:, kc * 128:(kc + 1) * 128], Wn[:, kc * 128:(kc + 1) * 128]), waits=["dv", "pe"], incs=[wch], dma=True)

                    def av_emit(h=h, g=g, nkc=nkc, wch=wch):
                        for kc in range(nkc):
                            w0 = [wch, "ac"] if kc == 0 else []
                            p.op("tensor", lambda e, kc=kc, g=g, nkc=nkc: e.matmul(ps32[0:64, 2048:2176], V[:, kc * 256 + g * 64: kc * 256 + (g + 1) * 64], WT[:, kc * 128:(kc + 1) * 128], start=(kc == 0), stop=(kc == nkc - 1)), waits=w0, incs=(["pe"] if kc == nkc - 1 else []))
                        p.op("scalar", lambda e, h=h: e.copy(CTXT[(h % 2) * 64:(h % 2) * 64 + 64, (h // 2) * 128:(h // 2) * 128 + 128], ps32[0:64, 2048:2176]), waits=["pe"], incs=["ac"])
                    hs["pending_av"] = av_emit

                if hs.get("pending_av") is not None:
                    hs["pending_av"]()
                    hs["pending_av"] = None

                def make_tail(t=t, rslot=rslot):
                    ac_t = p.cnt["ac"]
                    dv_t = p.cnt["dv"]
                    def tail():
                        for c in (range(8) if ph >= 4 else []):
                            w0 = [("ac", ac_t), ("dv", dv_t)] if c == 0 else []
                            p.op("tensor", lambda e, c=c: e.matmul(ps32[:, 2560:3072], CTXT[:, c * 128:(c + 1) * 128], wob[:, c * D:c * D + 512], start=(c == 0), stop=False), waits=w0)
                            p.op("tensor", lambda e, c=c: e.matmul(ps32[:, 3072:3584], CTXT[:, c * 128:(c + 1) * 128], wob[:, c * D + 512:(c + 1) * D], start=(c == 0), stop=False))
                        fb = None
                        if ph >= 4:
                            p.op("tensor", lambda e: e.matmul(ps32[:, 2560:3072], onescb, bor[:, 0:512], start=False, stop=True))
                            p.op("tensor", lambda e: e.matmul(ps32[:, 3072:3584], onescb, bor[:, 512:1024], start=False, stop=True), incs=["pe"])
                            if dbg and t == 0:
                                p.op("gpsimd", lambda e: e.dma_start(out=dbg_t["dctx"], in_=CTXT[:, :]), waits=["ac"], incs=["dbg"], dma=True)
                                p.op("vector", lambda e: e.engine_nop(), waits=["dbg"], incs=["dv"])
                            p.op("vector", lambda e, rslot=rslot: e.scalar_tensor_tensor(out=rslot, in0=ps32[:, 2560:3584], scalar=1.0, in1=rslot, op0=AL.mult, op1=AL.add), waits=["pe", "dv"], incs=["dv"])
                            if dbg and t == 0:
                                p.op("gpsimd", lambda e: e.dma_start(out=dbg_t["dres1"], in_=rslot), waits=["dv"], incs=["dbg"], dma=True)
                                p.op("vector", lambda e: e.engine_nop(), waits=["dbg"], incs=["dv"])
                        dv_ae = p.cnt["dv"]
                        if ph >= 5:
                            fb = ffn(rslot, tap=(dbg and t == 0), defer_lora=True, dv_ae=dv_ae)
                        if dbg and t == 0:
                            p.op("gpsimd", lambda e: e.dma_start(out=dbg_t["dres2"], in_=rslot), waits=["dv"], incs=["dbg"], dma=True)
                            p.op("vector", lambda e: e.engine_nop(), waits=["dbg"], incs=["dv"])
                        return fb
                    return tail

                if slot == 3 or ph < 2:
                    fb = make_tail()()
                    if fb is not None:
                        fb()
                    if ph < 2 and slot != 3 and t + 1 < nt:
                        prep(t + 1)()
                else:
                    pend["tail"] = make_tail()

                if slot == 3:
                    G = t // 4
                    for tt in (range(4) if ph >= 6 else []):
                        rs = res[:, tt * D:(tt + 1) * D]
                        p.op("vector", lambda e, rs=rs: e.reduce_sum(out=S1, in_=rs, axis=AX.X), waits=["dv", "ac"], incs=["dv"])
                        p.op("scalar", lambda e, rs=rs: e.activation(out=ps32[:, 2560:3584], in_=rs, func=AF.Square, accum_out=SQ), waits=["ac", "dv"], incs=["ac"])
                        p.op("vector", lambda e: e.tensor_scalar(out=MEAN, in0=S1, scalar1=1.0 / D, scalar2=None, op0=AL.mult), waits=["dv"], incs=["dv"])
                        p.op("vector", lambda e: e.tensor_scalar(out=MSQ, in0=SQ, scalar1=1.0 / D, scalar2=None, op0=AL.mult), waits=["ac", "dv"], incs=["dv"])
                        p.op("vector", lambda e: e.tensor_mul(M2, MEAN, MEAN), waits=["dv"], incs=["dv"])
                        p.op("vector", lambda e: e.tensor_sub(VAR, MSQ, M2), waits=["dv"], incs=["dv"])
                        p.op("scalar", lambda e: e.activation(out=STD, in_=VAR, func=AF.Sqrt, bias=cb[:, 0:1], scale=1.0), waits=["dv"], incs=["ac"])
                        p.op("vector", lambda e: e.reciprocal(RSTD, STD), waits=["ac"], incs=["dv"])
                        p.op("vector", lambda e: e.tensor_mul(T2, VAR, RSTD), waits=["dv"], incs=["dv"])
                        p.op("vector", lambda e: e.tensor_mul(T2, T2, RSTD), waits=["dv"], incs=["dv"])
                        p.op("scalar", lambda e: e.activation(out=T0, in_=T2, func=AF.Sqrt, bias=cb[:, 0:1], scale=1.0), waits=["dv"], incs=["ac"])
                        p.op("vector", lambda e: e.reciprocal(T2, T0), waits=["ac"], incs=["dv"])
                        p.op("vector", lambda e, tt=tt: e.tensor_scalar(out=ONEP[:, tt:tt + 1], in0=RSTD, scalar1=1.0, scalar2=None, op0=AL.add), waits=["dv"], incs=["dv"])
                        p.op("vector", lambda e, tt=tt: e.tensor_mul(NMR[:, tt:tt + 1], MEAN, RSTD), waits=["dv"], incs=["dv"])
                        p.op("vector", lambda e, tt=tt: e.tensor_scalar(out=NMR[:, tt:tt + 1], in0=NMR[:, tt:tt + 1], scalar1=-1.0, scalar2=None, op0=AL.mult), waits=["dv"], incs=["dv"])
                        p.op("vector", lambda e: e.tensor_mul(T0, RSTD, T2), waits=["dv"], incs=["dv"])
                        p.op("vector", lambda e: e.tensor_mul(NEGB, MEAN, T0), waits=["dv"], incs=["dv"])
                        p.op("vector", lambda e: e.tensor_scalar(out=NEGB, in0=NEGB, scalar1=-1.0, scalar2=None, op0=AL.mult), waits=["dv"], incs=["dv"])
                        p.op("scalar", lambda e, rs=rs: e.activation(out=xn_bf[:, :], in_=rs, func=AF.Identity, bias=NEGB, scale=T0), waits=["dv"], incs=["ac"])
                        for c in range(8):
                            p.op("tensor", lambda e, c=c: e.transpose(psbf[:, c * 128:(c + 1) * 128], xn_bf[:, c * 128:(c + 1) * 128], identb[:, :]), waits=["ac", "dv"], incs=(["pe"] if c == 7 else []))
                        for c in range(8):
                            p.op("scalar", lambda e, c=c, tt=tt: e.copy(xn3T[:, c * 512 + tt * 128: c * 512 + (tt + 1) * 128], psbf[:, c * 128:(c + 1) * 128]), waits=["pe"], incs=(["ac"] if c == 7 else []))
                    for ncx in (range(8) if ph >= 6 else []):
                        b0 = ((2 * ncx) % 3) * 2 * D
                        b1 = ((2 * ncx + 1) % 3) * 2 * D
                        for ph01, wis in ((0, (0, 3)), (1, (1, 2))):
                            slot = (2 * ncx + ph01) % 3
                            base = slot * 2 * D
                            for k2, wi in enumerate(wis):
                                p.op("sync", lambda e, wi=wi, ncx=ncx, base=base, k2=k2, slot=slot: e.dma_start(
                                    out=ssw[:, base + k2 * D: base + (k2 + 1) * D].rearrange("p (dc c) -> p dc c", c=128),
                                    in_=wt_scr[wi][:, ncx * 128:(ncx + 1) * 128].rearrange("(dc p) c -> p dc c", p=128)), waits=[("pe", ssw_war[slot]), ("ssw", p0_state["ssw"])], incs=["sw%d" % slot], dma=True)
                        for k2 in range(2):
                            for dc in range(8):
                                w0 = ["sw%d" % ((2 * ncx) % 3), "ac", "dv"] if (k2 == 0 and dc == 0) else []
                                reg = 2560 if k2 == 0 else 3072
                                p.op("tensor", lambda e, k2=k2, dc=dc, b0=b0, reg=reg: e.matmul(ps32[:, reg:reg + 512], ssw[:, b0 + k2 * D + dc * 128: b0 + k2 * D + (dc + 1) * 128], xn3T[:, dc * 512:(dc + 1) * 512], start=(dc == 0), stop=(dc == 7)), waits=w0, incs=(["pe"] if (k2 == 1 and dc == 7) else []))
                        ssw_war[(2 * ncx) % 3] = p.cnt["pe"]
                        p.op("scalar", lambda e, ncx=ncx: e.activation(out=asg[:, :], in_=ps32[:, 2560:3072], func=AF.Sigmoid, bias=abt[:, ncx:ncx + 1], scale=1.0), waits=["pe"], incs=["ac"])
                        p.op("scalar", lambda e: e.activation(out=dsg[:, :], in_=ps32[:, 3072:3584], func=AF.Sigmoid), incs=["ac"])
                        for k2 in range(2):
                            for dc in range(8):
                                w0 = ["sw%d" % ((2 * ncx + 1) % 3), ("ac", p.cnt["ac"])] if (k2 == 0 and dc == 0) else []
                                reg = 2560 if k2 == 0 else 3072
                                p.op("tensor", lambda e, k2=k2, dc=dc, b1=b1, reg=reg: e.matmul(ps32[:, reg:reg + 512], ssw[:, b1 + k2 * D + dc * 128: b1 + k2 * D + (dc + 1) * 128], xn3T[:, dc * 512:(dc + 1) * 512], start=(dc == 0), stop=(dc == 7)), waits=w0, incs=(["pe"] if (k2 == 1 and dc == 7) else []))
                        ssw_war[(2 * ncx + 1) % 3] = p.cnt["pe"]
                        p.op("scalar", lambda e: e.activation(out=csg[:, :], in_=ps32[:, 3072:3584], func=AF.Sigmoid), waits=["pe"], incs=["ac"])
                        p.op("vector", lambda e: e.tensor_mul(dbt[:, :], ps32[:, 2560:3072], dsg[:, :]), waits=["ac", "dv"], incs=["dv"])
                        p.op("vector", lambda e, ncx=ncx: e.tensor_tensor_scan(out=hh[:, :], data0=asg[:, :], data1=dbt[:, :], initial=carry[:, ncx:ncx + 1], op0=AL.mult, op1=AL.add), waits=["dv"], incs=["dv"])
                        p.op("vector", lambda e, ncx=ncx: e.tensor_copy(carry[:, ncx:ncx + 1], hh[:, 511:512]), waits=["dv"], incs=["dv"])
                        p.op("vector", lambda e, ncx=ncx: e.tensor_mul(Hp[:, ncx * 512:(ncx + 1) * 512], csg[:, :], hh[:, :]), waits=["dv"], incs=["dv"])
                    for tt in range(4):
                        rs = res[:, tt * D:(tt + 1) * D]
                        for ncx in (range(8) if ph >= 6 else []):
                            w0 = ["dv", "ac"] if ncx == 0 else []
                            p.op("tensor", lambda e, ncx=ncx, tt=tt: e.matmul(ps32[:, 2560:3072], Hp[:, ncx * 512 + tt * 128: ncx * 512 + (tt + 1) * 128], outwT[:, ncx * 1024: ncx * 1024 + 512], start=(ncx == 0), stop=False), waits=w0)
                            p.op("tensor", lambda e, ncx=ncx, tt=tt: e.matmul(ps32[:, 3072:3584], Hp[:, ncx * 512 + tt * 128: ncx * 512 + (tt + 1) * 128], outwT[:, ncx * 1024 + 512:(ncx + 1) * 1024], start=(ncx == 0), stop=False))
                        if ph >= 6:
                            p.op("tensor", lambda e: e.matmul(ps32[:, 2560:3072], onescb, obr[:, 0:512], start=False, stop=True))
                            p.op("tensor", lambda e: e.matmul(ps32[:, 3072:3584], onescb, obr[:, 512:1024], start=False, stop=True), incs=["pe"])
                            p.op("vector", lambda e, rs=rs, tt=tt: e.tensor_scalar(out=rs, in0=rs, scalar1=ONEP[:, tt:tt + 1], scalar2=NMR[:, tt:tt + 1], op0=AL.mult, op1=AL.add), waits=["pe", "dv"], incs=["dv"])
                            p.op("vector", lambda e, rs=rs: e.scalar_tensor_tensor(out=rs, in0=ps32[:, 2560:3584], scalar=1.0, in1=rs, op0=AL.mult, op1=AL.add), waits=["dv"], incs=["dv"])
                        if dbg and G == 0 and tt == 0:
                            p.op("gpsimd", lambda e, rs=rs: e.dma_start(out=dbg_t["dres3"], in_=rs), waits=["dv"], incs=["dbg"], dma=True)
                            p.op("vector", lambda e: e.engine_nop(), waits=["dbg"], incs=["dv"])
                        if ph >= 7:
                            ffn(rs)
                        p.op("sync", lambda e, G=G, tt=tt, rs=rs: e.dma_start(out=y_d[(G * 4 + tt) * 128:(G * 4 + tt + 1) * 128, :], in_=rs), waits=["dv"], incs=["out"], dma=True)
                    if t + 1 < nt:
                        prep(t + 1)()

        for name, deco in [("sync", block.sync), ("tensor", block.tensor),
                           ("scalar", block.scalar), ("vector", block.vector),
                           ("gpsimd", block.gpsimd)]:
            def body(eng, name=name):
                p.begin(name, eng)
                sched()
                if name == "sync":
                    eng.wait_ge(sems["out"], nt * 16)
            deco(body)

    return nc


_NC_CACHE = None


def kernel(**inputs):
    global _NC_CACHE
    xs = np.ascontiguousarray(np.asarray(inputs["L0_attn_a_x"], dtype=np.float32))
    B = xs.shape[0]
    wmap = {
        "wq": "L0_attn_a_wq", "bq": "L0_attn_a_bq", "wkv": "L0_attn_a_wkv",
        "bkv": "L0_attn_a_bkv", "wo": "L0_attn_a_wo", "bo": "L0_attn_a_bo",
        "qn": "L0_attn_a_qn_w", "kn": "L0_attn_a_kn_w", "nw": "L0_attn_b_norm_w",
        "aw": "L0_attn_b_A_w", "abias": "L0_attn_b_A_b", "bw": "L0_attn_b_B_w",
        "cw": "L0_attn_b_C_w", "dw": "L0_attn_b_D_w", "ow": "L0_attn_b_out_w",
        "obias": "L0_attn_b_out_b", "fk": "L0_ffn_kernel", "fb": "L0_ffn_bias",
        "fa": "L0_ffn_a", "fbm": "L0_ffn_b",
    }
    shared = {k: np.ascontiguousarray(np.asarray(inputs[v], dtype=np.float32))
              for k, v in wmap.items()}
    if _NC_CACHE is None:
        _NC_CACHE = build_nc()
    nc = _NC_CACHE
    in_maps = [dict(shared, x=np.ascontiguousarray(xs[b])) for b in range(B)]
    r = run_bass_kernel_spmd(nc, in_maps, core_ids=list(range(B)))
    out = np.stack([np.asarray(r.results[b]["y"], dtype=np.float32) for b in range(B)])
    return out


# version 18 (final)
# speedup vs baseline: 1.2271x; 1.0224x over previous
import numpy as np
import ml_dtypes
import concourse.bass as bass
import concourse.mybir as mybir
from concourse.bass_utils import run_bass_kernel_spmd
from contextlib import ExitStack

F32 = mybir.dt.float32
BF16 = mybir.dt.bfloat16
I32 = mybir.dt.int32
AL = mybir.AluOpType
AF = mybir.ActivationFunctionType
AX = mybir.AxisListType

T = 4096
D = 1024
H = 16
NT = 32
RIMM = -2.0e30
FLAGT = -1.5e30
NEG = -1.0e30


def _consts():
    tri = np.zeros((128, 128), dtype=np.float32)
    tri[np.triu_indices(128, 1)] = NEG
    j = np.arange(T, dtype=np.float64)
    j_hi = np.floor(j / 32.0) * 32.0
    j_lo = j - j_hi
    j4 = np.stack([j_hi, j_lo, j_hi, j_lo]).astype(np.float32)
    slopes = (2.0 ** (-(np.arange(H, dtype=np.float64) / (H - 1)) * 8.0)).astype(np.float32)
    s_hi = slopes.astype(ml_dtypes.bfloat16).astype(np.float32)
    s_lo = (slopes - s_hi).astype(np.float32)
    sl4 = np.zeros((4, H * 128), dtype=np.float32)
    for h in range(H):
        sl4[0, h * 128:(h + 1) * 128] = s_hi[h]
        sl4[1, h * 128:(h + 1) * 128] = s_hi[h]
        sl4[2, h * 128:(h + 1) * 128] = s_lo[h]
        sl4[3, h * 128:(h + 1) * 128] = s_lo[h]
    icol = np.arange(128, dtype=np.float32).reshape(128, 1)
    return (tri, j4.astype(ml_dtypes.bfloat16), sl4.astype(ml_dtypes.bfloat16),
            np.eye(128, dtype=ml_dtypes.bfloat16), icol)


class P:

    def __init__(self, nc, sems):
        self.nc = nc
        self.sems = sems
        self.cnt = None
        self.eng_name = None
        self.eng = None
        self.last_wait = None
        self.tags = {}
        self.eng_pc = 0

    def begin(self, eng_name, eng):
        self.eng_name = eng_name
        self.eng = eng
        self.cnt = {k: 0 for k in self.sems}
        self.last_wait = {k: 0 for k in self.sems}
        self.tags[eng_name] = []
        self.eng_pc = 0

    def op(self, eng, fn, waits=(), incs=(), dma=False):
        if eng == self.eng_name:
            import sys as _sys
            _ln = _sys._getframe(1).f_lineno
            _i0 = self.nc.n_instructions()
            for w in waits:
                if isinstance(w, tuple):
                    w, v = w
                else:
                    v = self.cnt[w]
                if v > 0 and v > self.last_wait[w]:
                    self.eng.wait_ge(self.sems[w], v)
                    self.last_wait[w] = v
            inst = fn(self.eng)
            _n = self.nc.n_instructions() - _i0
            self.tags[self.eng_name].append((self.eng_pc, self.eng_pc + _n, _ln))
            self.eng_pc += _n
            amt = 16 if dma else 1
            if incs:
                assert inst is not None
                for s in incs:
                    inst.then_inc(self.sems[s], amt)
        amt = 16 if dma else 1
        for s in incs:
            self.cnt[s] += amt


_last_p = [None]


def build_nc(nt=NT, dbg=False, ph=99, pass_cap=2048):
    Tl = 128 * nt
    nc = bass.Bass("TRN2", target_bir_lowering=False)

    x_d = nc.dram_tensor("x", [Tl, D], F32, kind="ExternalInput").ap()
    y_d = nc.dram_tensor("y", [Tl, D], F32, kind="ExternalOutput").ap()
    wq_d = nc.dram_tensor("wq", [D, D], F32, kind="ExternalInput").ap()
    bq_d = nc.dram_tensor("bq", [D], F32, kind="ExternalInput").ap()
    wkv_d = nc.dram_tensor("wkv", [D, 512], F32, kind="ExternalInput").ap()
    bkv_d = nc.dram_tensor("bkv", [512], F32, kind="ExternalInput").ap()
    wo_d = nc.dram_tensor("wo", [D, D], F32, kind="ExternalInput").ap()
    bo_d = nc.dram_tensor("bo", [D], F32, kind="ExternalInput").ap()
    qn_d = nc.dram_tensor("qn", [64], F32, kind="ExternalInput").ap()
    kn_d = nc.dram_tensor("kn", [64], F32, kind="ExternalInput").ap()
    nw_d = nc.dram_tensor("nw", [D], F32, kind="ExternalInput").ap()
    aw_d = nc.dram_tensor("aw", [D, D], F32, kind="ExternalInput").ap()
    abias_d = nc.dram_tensor("abias", [D], F32, kind="ExternalInput").ap()
    bw_d = nc.dram_tensor("bw", [D, D], F32, kind="ExternalInput").ap()
    cw_d = nc.dram_tensor("cw", [D, D], F32, kind="ExternalInput").ap()
    dw_d = nc.dram_tensor("dw", [D, D], F32, kind="ExternalInput").ap()
    ow_d = nc.dram_tensor("ow", [D, D], F32, kind="ExternalInput").ap()
    obias_d = nc.dram_tensor("obias", [D], F32, kind="ExternalInput").ap()
    fk_d = nc.dram_tensor("fk", [D, D], F32, kind="ExternalInput").ap()
    fb_d = nc.dram_tensor("fb", [D], F32, kind="ExternalInput").ap()
    fa_d = nc.dram_tensor("fa", [D, 16], F32, kind="ExternalInput").ap()
    fbm_d = nc.dram_tensor("fbm", [16, D], F32, kind="ExternalInput").ap()

    tri_c, j4_c, sl4_c, identb_c, icol_c = _consts()
    icol_d = nc.inline_tensor(icol_c, "icol_c").ap()
    tri_d = nc.inline_tensor(tri_c, "tri_c").ap()
    j4_d = nc.inline_tensor(j4_c, "j4_c").ap()
    sl4_d = nc.inline_tensor(sl4_c, "sl4_c").ap()
    identb_d = nc.inline_tensor(identb_c, "identb_c").ap()

    wt_scr = [nc.dram_tensor(f"wtscr{i}", [D, D], BF16, kind="Internal").ap()
              for i in range(4)]
    gscr = nc.dram_tensor("gscr", [128, 1], F32, kind="Internal").ap()
    dbg_t = {}
    if dbg:
        for nm, shp in [("dq", [128, D]), ("dkv", [128, 512]), ("dqh", [128, D]),
                        ("dkh", [128, 256]), ("ds", [128, 128]), ("dwb", [128, 128]),
                        ("dT8", [128, 64]), ("dwn", [128, 128]), ("dctx", [128, D]),
                        ("dres1", [128, D]), ("dres2", [128, D]), ("dres3", [128, D]),
                        ("dffnq", [128, D]), ("dxn2", [128, D]), ("dscl", [128, 2])]:
            dbg_t[nm] = nc.dram_tensor(nm, shp, F32, kind="ExternalOutput").ap()

    es = ExitStack()
    sb = lambda n, s, dt: es.enter_context(nc.sbuf_tensor(n, s, dt))
    with es:
        wqb = sb("wqb", [128, 8 * D], BF16)
        wkvb = sb("wkvb", [128, 8 * 512], BF16)
        wob = sb("wob", [128, 8 * D], BF16)
        ffnq = sb("ffnq", [128, 8 * D], BF16)
        ab = sb("ab", [128, 8 * 16], BF16)
        outwT = sb("outwT", [128, 8 * D], BF16)
        ssw = sb("ssw", [128, 6 * D], BF16)
        kt = [sb("kt0", [128, Tl], BF16), sb("kt1", [128, Tl], BF16)]
        V = sb("V", [128, nt * 256], BF16)
        res = sb("res", [128, 4 * D], F32)
        s_sb = sb("s_sb", [128, max(Tl, 4096)], F32)
        Wb = sb("Wbuf", [128, max(Tl, 4096)], BF16)
        Wn = sb("Wnorm", [128, max(Tl, 4096)], BF16)
        WT = sb("WT", [128, max(Tl, 4096)], BF16)
        qTb = [sb("qT0", [128, D], BF16), sb("qT1", [128, D], BF16)]
        xnT = sb("xnT", [128, D], BF16)
        xn_bf = sb("xn_bf", [128, D], BF16)
        CTXT = sb("CTXT", [128, D], BF16)
        q16 = sb("q16", [128, D], BF16)
        kv16 = sb("kv16", [128, 512], BF16)
        xn3T = WT
        Hp = Wb
        asg = sb("asg", [128, 512], F32)
        dsg = sb("dsg", [128, 512], F32)
        csg = sb("csg", [128, 512], BF16)
        dbt = dsg
        hh = dbt
        QN8B = sb("QN8B", [128, 64], BF16)
        KNB = sb("KNB", [128, 64], BF16)
        tri = sb("tri", [128, 128], F32)
        identb = sb("identb", [128, 128], BF16)
        sh = sb("sh", [128, 6144], BF16)
        j4b = sh[0:4, 0:T]
        sl4 = sh[0:4, 4096:4096 + H * 128]
        onescb = sh[64:65, 5888:6016]
        bqr = sh[64:65, 0:D]
        bor = sh[64:65, 1024:2048]
        fbr = sh[64:65, 2048:3072]
        obr = sh[64:65, 3072:4096]
        bkvr = sh[64:65, 4096:4608]
        qn8r = sh[64:65, 4608:5632]
        knr = sh[64:65, 5632:5888]
        onescf = s_sb[0:1, 3200:3328]
        rowf = s_sb[0:1, 2048:3072]
        rowf2 = s_sb[0:1, 3072:4096]
        fbm32 = s_sb[0:16, 3072:4096]
        abt = sb("abt", [128, 8], F32)
        nwt = sb("nwt", [128, 8], F32)
        T8 = sb("T8", [128, 64], F32)
        sm = sb("sm", [128, 64], F32)
        carry = sb("carry", [128, 8], F32)
        cb = sb("cb", [128, 2], F32)
        lb0 = sb("lb0", [16, 1152], BF16)
        l1b = lb0[0:16, 0:128]
        bb = lb0[0:16, 128:1152]
        icol = sb("icol", [128, 1], F32)
        RB = sb("RB", [128, 2], F32)

        ps32 = es.enter_context(nc.psum_tensor("ps32", [128, 3584], F32))
        psbf = es.enter_context(nc.psum_tensor("psbf", [128, 1024], BF16))

        sem_names = ["ld", "pe", "ac", "dv", "wt0", "wt1", "wt2", "wt3", "wt4", "wt5", "wt6", "wt7", "out", "ssw", "sw0", "sw1", "sw2", "dbg"]
        sems = {n: es.enter_context(nc.semaphore(n)) for n in sem_names}
        block = es.enter_context(nc.Block())

        p = P(nc, sems)

        _last_p[0] = p

        S1 = sm[:, 0:1]; SQ = sm[:, 1:2]; MEAN = sm[:, 2:3]; MSQ = sm[:, 3:4]
        M2 = sm[:, 4:5]; VAR = sm[:, 5:6]; STD = sm[:, 6:7]; RSTD = sm[:, 7:8]
        T0 = sm[:, 8:9]; NEGB = sm[:, 9:10]; Zc = sm[:, 10:11]; RZ = sm[:, 11:12]
        GMAX = sm[:, 12:13]; SCL = sm[:, 13:14]; RSCL = sm[:, 14:15]
        RED16 = sm[:, 16:32]; QRS = sm[:, 32:48]
        RED4 = sm[:, 48:52]; KRS = sm[:, 52:56]
        T2 = sm[:, 15:16]
        ONEP = sm[:, 56:60]; NMR = sm[:, 60:64]

        def sched():
            ssw_war = {0: 0, 1: 0, 2: 0}
            p0_state = {}

            p.op("sync", lambda e: e.dma_start(out=tri[:], in_=tri_d), incs=["ld"], dma=True)
            p.op("sync", lambda e: e.dma_start(out=j4b, in_=j4_d), incs=["ld"], dma=True)
            p.op("sync", lambda e: e.dma_start(out=sl4, in_=sl4_d), incs=["ld"], dma=True)
            p.op("sync", lambda e: e.dma_start(out=identb[:], in_=identb_d), incs=["ld"], dma=True)
            p.op("sync", lambda e: e.dma_start(out=icol[:], in_=icol_d), incs=["ld"], dma=True)
            for c in range(8):
                p.op("sync", lambda e, c=c: e.dma_start(out=abt[:, c:c + 1], in_=abias_d[c * 128:(c + 1) * 128].rearrange("(p one) -> p one", one=1)), incs=["ld"], dma=True)
                p.op("sync", lambda e, c=c: e.dma_start(out=nwt[:, c:c + 1], in_=nw_d[c * 128:(c + 1) * 128].rearrange("(p one) -> p one", one=1)), incs=["ld"], dma=True)
            p.op("vector", lambda e: e.memset(onescb, 1.0), incs=["dv"])
            p.op("vector", lambda e: e.memset(onescf[:], 1.0), incs=["dv"])
            p.op("vector", lambda e: e.memset(carry[:], 0.0), incs=["dv"])
            p.op("vector", lambda e: e.memset(cb[:, 0:1], 1e-5), incs=["dv"])
            p.op("vector", lambda e: e.memset(cb[:, 1:2], 1e-8), incs=["dv"])

            def rowload(dst_bf, src_ap, n):
                p.op("sync", lambda e: e.dma_start(out=rowf[:, 0:n], in_=src_ap.rearrange("(one n) -> one n", one=1)), waits=["dv"], incs=["ld"], dma=True)
                p.op("vector", lambda e: e.tensor_copy(dst_bf[:, 0:n], rowf[:, 0:n]), waits=["ld"], incs=["dv"])

            rowload(bqr, bq_d, D)
            rowload(bkvr, bkv_d, 512)
            rowload(bor, bo_d, D)
            rowload(obr, obias_d, D)

            for hr in range(16):
                p.op("sync", lambda e, hr=hr: e.dma_start(out=rowf2[:, hr * 64:(hr + 1) * 64], in_=qn_d.rearrange("(one n) -> one n", one=1)), waits=(["dv"] if hr == 0 else []), incs=["ld"], dma=True)
            p.op("vector", lambda e: e.tensor_scalar(out=rowf2, in0=rowf2, scalar1=0.125, scalar2=None, op0=AL.mult), waits=["ld"], incs=["dv"])
            p.op("vector", lambda e: e.tensor_copy(qn8r, rowf2[:, :]), waits=["dv"], incs=["dv"])
            p.op("tensor", lambda e: e.matmul(ps32[:, 0:64], onescb, qn8r[:, 0:64], start=True, stop=True), waits=["dv"], incs=["pe"])
            p.op("scalar", lambda e: e.copy(QN8B[:, :], ps32[:, 0:64]), waits=["pe"], incs=["ac"])
            for rr in range(4):
                p.op("sync", lambda e, rr=rr: e.dma_start(out=rowf2[:, rr * 64:(rr + 1) * 64], in_=kn_d.rearrange("(one n) -> one n", one=1)), waits=["ac"], incs=["ld"], dma=True)
            p.op("vector", lambda e: e.tensor_copy(knr, rowf2[:, 0:256]), waits=["ld"], incs=["dv"])
            p.op("tensor", lambda e: e.matmul(ps32[:, 1024:1088], onescb, knr[:, 0:64], start=True, stop=True), waits=["dv"], incs=["pe"])
            p.op("scalar", lambda e: e.copy(KNB[:, :], ps32[:, 1024:1088]), waits=["pe"], incs=["ac"])

            def wstream(dst, dst_off, src_ap, cols, slab_i):
                sl = slab_i % 2
                stg = s_sb[:, sl * 1024: sl * 1024 + cols]
                p.op("sync", lambda e, stg=stg, src_ap=src_ap: e.dma_start(out=stg, in_=src_ap), waits=["dv"], incs=["ld"], dma=True)
                p.op("vector", lambda e, stg=stg, dst=dst, dst_off=dst_off, cols=cols: e.tensor_copy(dst[:, dst_off:dst_off + cols], stg), waits=["ld"], incs=["dv"])

            si = 0
            for c in range(8):
                wstream(wqb, c * D, wq_d[c * 128:(c + 1) * 128, :], D, si); si += 1
            for c in range(8):
                wstream(wkvb, c * 512, wkv_d[c * 128:(c + 1) * 128, :], 512, si); si += 1
            for c in range(8):
                wstream(wob, c * D, wo_d[c * 128:(c + 1) * 128, :], D, si); si += 1
            for c in range(8):
                wstream(ab, c * 16, fa_d[c * 128:(c + 1) * 128, :], 16, si); si += 1

            for c in range(8):
                stg = s_sb[:, (c % 2) * 1024:(c % 2) * 1024 + D]
                p.op("sync", lambda e, stg=stg, c=c: e.dma_start(out=stg, in_=fk_d[c * 128:(c + 1) * 128, :]), waits=["dv"], incs=["ld"], dma=True)
                p.op("vector", lambda e, stg=stg, c=c: e.tensor_reduce(out=T8[:, c:c + 1], in_=stg, axis=AX.X, op=AL.max), waits=["ld"], incs=["dv"])
                p.op("vector", lambda e, stg=stg, c=c: e.tensor_reduce(out=T8[:, 8 + c:9 + c], in_=stg, axis=AX.X, op=AL.min), waits=["dv"], incs=["dv"])
            p.op("vector", lambda e: e.tensor_reduce(out=GMAX, in_=T8[:, 0:8], axis=AX.X, op=AL.max), waits=["dv"], incs=["dv"])
            p.op("vector", lambda e: e.tensor_reduce(out=sm[:, 15:16], in_=T8[:, 8:16], axis=AX.X, op=AL.min), waits=["dv"], incs=["dv"])
            p.op("vector", lambda e: e.tensor_scalar(out=sm[:, 15:16], in0=sm[:, 15:16], scalar1=-1.0, scalar2=None, op0=AL.mult), waits=["dv"], incs=["dv"])
            p.op("vector", lambda e: e.tensor_max(GMAX, GMAX, sm[:, 15:16]), waits=["dv"], incs=["dv"])
            p.op("vector", lambda e: e.memset(onescf, 1.0), waits=["dv"], incs=["dv"])
            p.op("sync", lambda e: e.dma_start(out=gscr, in_=GMAX), waits=["dv"], incs=["ld"], dma=True)
            p.op("sync", lambda e: e.dma_start(out=rowf[0:1, 0:128], in_=gscr.rearrange("p o -> o p")), waits=["ld"], incs=["ld"], dma=True)
            p.op("vector", lambda e: e.tensor_reduce(out=sm[0:1, 13:14], in_=rowf[0:1, 0:128], axis=AX.X, op=AL.max), waits=["ld"], incs=["dv"])
            p.op("vector", lambda e: e.tensor_copy(rowf[0:1, 0:1], sm[0:1, 13:14]), waits=["dv"], incs=["dv"])
            p.op("tensor", lambda e: e.matmul(ps32[:, 1664:1665], onescf, rowf[0:1, 0:1], start=True, stop=True), waits=["dv"], incs=["pe"])
            p.op("scalar", lambda e: e.copy(GMAX, ps32[:, 1664:1665]), waits=["pe"], incs=["ac"])
            p.op("vector", lambda e: e.tensor_scalar(out=SCL, in0=GMAX, scalar1=1.0 / 7.0, scalar2=None, op0=AL.mult), waits=["ac"], incs=["dv"])
            p.op("vector", lambda e: e.reciprocal(RSCL, SCL), waits=["dv"], incs=["dv"])
            for c in range(8):
                stg = s_sb[:, (c % 2) * 1024:(c % 2) * 1024 + D]
                p.op("sync", lambda e, stg=stg, c=c: e.dma_start(out=stg, in_=fk_d[c * 128:(c + 1) * 128, :]), waits=["dv"], incs=["ld"], dma=True)
                p.op("vector", lambda e, stg=stg: e.tensor_scalar(out=stg, in0=stg, scalar1=RSCL, scalar2=None, op0=AL.mult), waits=["ld"], incs=["dv"])
                p.op("vector", lambda e, stg=stg: e.tensor_scalar(out=stg, in0=stg, scalar1=12582912.0, scalar2=-12582912.0, op0=AL.add, op1=AL.add), waits=["dv"], incs=["dv"])
                p.op("vector", lambda e, stg=stg: e.tensor_scalar(out=stg, in0=stg, scalar1=-8.0, scalar2=7.0, op0=AL.max, op1=AL.min), waits=["dv"], incs=["dv"])
                p.op("vector", lambda e, stg=stg, c=c: e.tensor_copy(ffnq[:, c * D:(c + 1) * D], stg), waits=["dv"], incs=["dv"])
            if dbg:
                p.op("gpsimd", lambda e: e.dma_start(out=dbg_t["dffnq"], in_=ffnq[:, 0:D]), waits=["dv"], incs=["dbg"], dma=True)
                p.op("gpsimd", lambda e: e.dma_start(out=dbg_t["dscl"], in_=sm[:, 13:15]), waits=["dv"], incs=["dbg"], dma=True)
                p.op("vector", lambda e: e.engine_nop(), waits=["dbg"], incs=["dv"])
            p.op("sync", lambda e: e.dma_start(out=rowf[:, 0:D], in_=fb_d.rearrange("(one n) -> one n", one=1)), waits=["dv"], incs=["ld"], dma=True)
            p.op("vector", lambda e: e.tensor_scalar(out=rowf[:, 0:D], in0=rowf[:, 0:D], scalar1=RSCL[0:1, 0:1], scalar2=None, op0=AL.mult), waits=["ld"], incs=["dv"])
            p.op("vector", lambda e: e.tensor_copy(fbr[:, 0:D], rowf[:, 0:D]), waits=["dv"], incs=["dv"])
            p.op("sync", lambda e: e.dma_start(out=fbm32, in_=fbm_d), waits=["dv"], incs=["ld"], dma=True)
            p.op("vector", lambda e: e.tensor_scalar(out=fbm32, in0=fbm32, scalar1=RSCL[0:16, 0:1], scalar2=None, op0=AL.mult), waits=["ld"], incs=["dv"])
            p.op("vector", lambda e: e.tensor_copy(bb, fbm32), waits=["dv"], incs=["dv"])

            for wi, wd in enumerate([aw_d, bw_d, cw_d, dw_d]):
                for ncx in range(8):
                    sl = ncx % 2
                    stg = s_sb[:, sl * 1024: sl * 1024 + D]
                    stgb = Wb[:, sl * 1024: sl * 1024 + D]
                    p.op("sync", lambda e, stg=stg, wd=wd, ncx=ncx: e.dma_start(out=stg, in_=wd[ncx * 128:(ncx + 1) * 128, :]), waits=["dv", "ssw"], incs=["ld"], dma=True)
                    p.op("vector", lambda e, stg=stg, stgb=stgb: e.tensor_copy(stgb, stg), waits=["ld", "pe"], incs=["dv"])
                    for dc in range(8):
                        p.op("tensor", lambda e, stgb=stgb, dc=dc: e.transpose(psbf[:, dc * 128:(dc + 1) * 128], stgb[:, dc * 128:(dc + 1) * 128], identb[:, :]), waits=["dv", "ac"], incs=(["pe"] if dc == 7 else []))
                    for dc in range(8):
                        p.op("vector", lambda e, dc=dc: e.tensor_scalar(out=Wn[:, dc * 128:(dc + 1) * 128], in0=psbf[:, dc * 128:(dc + 1) * 128], scalar1=nwt[:, dc:dc + 1], scalar2=None, op0=AL.mult), waits=["pe"], incs=(["dv"] if dc == 7 else []))
                    p.op("sync", lambda e, wi=wi, ncx=ncx: e.dma_start(
                        out=wt_scr[wi][:, ncx * 128:(ncx + 1) * 128].rearrange("(dc p) c -> p dc c", p=128),
                        in_=Wn[:, 0:1024].rearrange("p (dc c) -> p dc c", c=128)), waits=["dv"], incs=["ssw"], dma=True)
            p0_state["ssw"] = p.cnt["ssw"]
            for dc in range(8):
                sl = dc % 2
                stg = s_sb[:, sl * 1024: sl * 1024 + D]
                stgb = Wb[:, sl * 1024: sl * 1024 + D]
                p.op("sync", lambda e, stg=stg, dc=dc: e.dma_start(out=stg, in_=ow_d[dc * 128:(dc + 1) * 128, :]), waits=["dv", "ssw"], incs=["ld"], dma=True)
                p.op("vector", lambda e, stg=stg, stgb=stgb: e.tensor_copy(stgb, stg), waits=["ld", "pe"], incs=["dv"])
                for ncb in range(8):
                    p.op("tensor", lambda e, stgb=stgb, ncb=ncb: e.transpose(psbf[:, ncb * 128:(ncb + 1) * 128], stgb[:, ncb * 128:(ncb + 1) * 128], identb[:, :]), waits=["dv", "ac"], incs=(["pe"] if ncb == 7 else []))
                for ncb in range(8):
                    p.op("scalar", lambda e, ncb=ncb, dc=dc: e.copy(outwT[:, ncb * 1024 + dc * 128: ncb * 1024 + (dc + 1) * 128], psbf[:, ncb * 128:(ncb + 1) * 128]), waits=["pe"], incs=(["ac"] if ncb == 7 else []))

            def layer_norm(src_ap, ldx=None, dvl=None):
                p.op("vector", lambda e: e.reduce_sum(out=S1, in_=src_ap, axis=AX.X), waits=(["dv", "ac"] if ldx is None else [("ld", ldx)]), incs=["dv"])
                p.op("scalar", lambda e: e.activation(out=ps32[:, 2560:3584], in_=src_ap, func=AF.Square, accum_out=SQ), waits=(["ac", "dv"] if ldx is None else [("ld", ldx), ("dv", dvl)]), incs=["ac"])
                p.op("vector", lambda e: e.tensor_scalar(out=MEAN, in0=S1, scalar1=1.0 / D, scalar2=None, op0=AL.mult), waits=["dv"], incs=["dv"])
                p.op("vector", lambda e: e.tensor_scalar(out=MSQ, in0=SQ, scalar1=1.0 / D, scalar2=None, op0=AL.mult), waits=["ac", "dv"], incs=["dv"])
                p.op("vector", lambda e: e.tensor_mul(M2, MEAN, MEAN), waits=["dv"], incs=["dv"])
                p.op("vector", lambda e: e.tensor_sub(VAR, MSQ, M2), waits=["dv"], incs=["dv"])
                p.op("scalar", lambda e: e.activation(out=STD, in_=VAR, func=AF.Sqrt, bias=cb[:, 0:1], scale=1.0), waits=["dv"], incs=["ac"])
                p.op("vector", lambda e: e.reciprocal(RSTD, STD), waits=["ac"], incs=["dv"])
                p.op("vector", lambda e: e.tensor_mul(T0, MEAN, RSTD), waits=["dv"], incs=["dv"])
                p.op("vector", lambda e: e.tensor_scalar(out=NEGB, in0=T0, scalar1=-1.0, scalar2=None, op0=AL.mult), waits=["dv"], incs=["dv"])
                p.op("scalar", lambda e: e.activation(out=xn_bf[:, :], in_=src_ap, func=AF.Identity, bias=NEGB, scale=RSTD), waits=["dv"], incs=["ac"])

            def transpose_to(dst, src, nblk, dst_stride=128, w0=None):
                for c in range(nblk):
                    p.op("tensor", lambda e, c=c: e.transpose(psbf[:, c * 128:(c + 1) * 128], src[:, c * 128:(c + 1) * 128], identb[:, :]), waits=((["ac", "dv"] if w0 is None else w0) if c == 0 else []), incs=(["pe"] if c == nblk - 1 else []))
                for c in range(nblk):
                    p.op("scalar", lambda e, c=c: e.copy(dst[:, c * dst_stride:c * dst_stride + 128], psbf[:, c * 128:(c + 1) * 128]), waits=["pe"], incs=(["ac"] if c == nblk - 1 else []))

            def ffn(rslot, tap=False, defer_lora=False, dv_ae=None):
                layer_norm(rslot)
                if tap:
                    p.op("gpsimd", lambda e: e.dma_start(out=dbg_t["dxn2"], in_=xn_bf[:, :]), waits=["ac"], incs=["dbg"], dma=True)
                    p.op("vector", lambda e: e.engine_nop(), waits=["dbg"], incs=["dv"])
                transpose_to(xnT, xn_bf, 8, w0=["ac"])
                if ph < 52:
                    return
                for c in range(8):
                    w0 = (["ac", "dv"] if dv_ae is None else ["ac", ("dv", dv_ae)]) if c == 0 else []
                    p.op("tensor", lambda e, c=c: e.matmul(ps32[0:16, 2560:2688], ab[:, c * 16:(c + 1) * 16], xnT[:, c * 128:(c + 1) * 128], start=(c == 0), stop=(c == 7)), waits=w0, incs=(["pe"] if c == 7 else []))
                p.op("scalar", lambda e: e.copy(l1b, ps32[0:16, 2560:2688]), waits=["pe"], incs=["ac"])
                ac_l1 = p.cnt["ac"]
                for c in range(8):
                    w0 = [("ac", ac_l1)] if c == 0 else []
                    p.op("tensor", lambda e, c=c: e.matmul(ps32[:, 2560:3072], xnT[:, c * 128:(c + 1) * 128], ffnq[:, c * D:c * D + 512], start=(c == 0), stop=False), waits=w0)
                    p.op("tensor", lambda e, c=c: e.matmul(ps32[:, 3072:3584], xnT[:, c * 128:(c + 1) * 128], ffnq[:, c * D + 512:(c + 1) * D], start=(c == 0), stop=False))
                p.op("tensor", lambda e: e.matmul(ps32[:, 2560:3072], onescb, fbr[:, 0:512], start=False, stop=True))
                p.op("tensor", lambda e: e.matmul(ps32[:, 3072:3584], onescb, fbr[:, 512:1024], start=False, stop=True), incs=["pe"])
                p.op("vector", lambda e, rslot=rslot: e.scalar_tensor_tensor(out=rslot, in0=ps32[:, 2560:3584], scalar=SCL, in1=rslot, op0=AL.mult, op1=AL.add), waits=["pe", "dv"], incs=["dv"])
                dv_y = p.cnt["dv"]

                def lora(dv_y=dv_y, rslot=rslot):
                    p.op("tensor", lambda e: e.matmul(ps32[:, 2560:3072], l1b, bb[:, 0:512], start=True, stop=True), waits=[("dv", dv_y)])
                    p.op("tensor", lambda e: e.matmul(ps32[:, 3072:3584], l1b, bb[:, 512:1024], start=True, stop=True), incs=["pe"])
                    p.op("vector", lambda e, rslot=rslot: e.scalar_tensor_tensor(out=rslot, in0=ps32[:, 2560:3584], scalar=SCL, in1=rslot, op0=AL.mult, op1=AL.add), waits=["pe", "dv"], incs=["dv"])
                if defer_lora:
                    return lora
                lora()

            def prep(t, ldx=None, dvl=None):
                rslot = res[:, (t % 4) * D:(t % 4 + 1) * D]
                if ldx is None:
                    p.op("sync", lambda e, t=t, rslot=rslot: e.dma_start(out=rslot, in_=x_d[t * 128:(t + 1) * 128, :]), waits=["out", "dv"], incs=["ld"], dma=True)
                    ldx = p.cnt["ld"]
                    dvl = p.cnt["dv"]

                layer_norm(rslot, ldx=ldx, dvl=dvl)
                transpose_to(xnT, xn_bf, 8, w0=["ac"])

                for c in range(8):
                    w0 = ["ac", ("dv", dvl)] if c == 0 else []
                    p.op("tensor", lambda e, c=c: e.matmul(ps32[:, 2560:3072], xnT[:, c * 128:(c + 1) * 128], wqb[:, c * D:c * D + 512], start=(c == 0), stop=False), waits=w0)
                    p.op("tensor", lambda e, c=c: e.matmul(ps32[:, 3072:3584], xnT[:, c * 128:(c + 1) * 128], wqb[:, c * D + 512:(c + 1) * D], start=(c == 0), stop=False))
                p.op("tensor", lambda e: e.matmul(ps32[:, 2560:3072], onescb, bqr[:, 0:512], start=False, stop=True))
                p.op("tensor", lambda e: e.matmul(ps32[:, 3072:3584], onescb, bqr[:, 512:1024], start=False, stop=True), incs=["pe"])
                p.op("scalar", lambda e: e.copy(q16[:, :], ps32[:, 2560:3584]), waits=["pe"], incs=["ac"])
                ac_q = p.cnt["ac"]
                for c in range(8):
                    w0 = [("ac", ac_q)] if c == 0 else []
                    p.op("tensor", lambda e, c=c: e.matmul(ps32[:, 2560:3072], xnT[:, c * 128:(c + 1) * 128], wkvb[:, c * 512:(c + 1) * 512], start=(c == 0), stop=False), waits=w0)
                p.op("tensor", lambda e: e.matmul(ps32[:, 2560:3072], onescb, bkvr[:, 0:512], start=False, stop=True), incs=["pe"])
                p.op("scalar", lambda e: e.copy(kv16[:, :], ps32[:, 2560:3072]), waits=["pe"], incs=["ac"])
                if dbg and t == 0:
                    p.op("gpsimd", lambda e: e.dma_start(out=dbg_t["dq"], in_=q16[:, :]), waits=["ac"], incs=["dbg"], dma=True)
                    p.op("gpsimd", lambda e: e.dma_start(out=dbg_t["dkv"], in_=kv16[:, :]), waits=["ac"], incs=["dbg"], dma=True)
                    p.op("vector", lambda e: e.engine_nop(), waits=["dbg"], incs=["dv"])

                def prep_b(t=t):
                    self_ = None
                    p.op("scalar", lambda e: e.activation(out=ps32[:, 2560:3584], in_=q16[:, :], func=AF.Square), waits=["ac"], incs=["ac"])
                    p.op("vector", lambda e: e.tensor_reduce(out=RED16, in_=ps32[:, 2560:3584].rearrange("p (h d) -> p h d", d=64), axis=AX.X, op=AL.add), waits=["ac"], incs=["dv"])
                    p.op("scalar", lambda e: e.activation(out=RED16, in_=RED16, func=AF.Sqrt, bias=cb[:, 1:2], scale=1.0 / 64.0), waits=["dv"], incs=["ac"])
                    p.op("vector", lambda e: e.reciprocal(QRS, RED16), waits=["ac"], incs=["dv"])
                    p.op("vector", lambda e: e.tensor_tensor(out=q16[:, :].rearrange("p (h d) -> p h d", d=64), in0=q16[:, :].rearrange("p (h d) -> p h d", d=64), in1=QRS.to_broadcast([128, 16, 64]), op=AL.mult), waits=["dv"], incs=["dv"])
                    p.op("vector", lambda e: e.tensor_tensor(out=xn_bf[:, :].rearrange("p (h d) -> p h d", d=64), in0=q16[:, :].rearrange("p (h d) -> p h d", d=64), in1=QN8B[:, :].rearrange("p (x d) -> p x d", x=1).to_broadcast([128, 16, 64]), op=AL.mult), waits=["dv", "pe"], incs=["dv"])
                    p.op("scalar", lambda e: e.activation(out=ps32[:, 2560:2816], in_=kv16[:, 0:256], func=AF.Square), waits=["dv"], incs=["ac"])
                    p.op("vector", lambda e: e.tensor_reduce(out=RED4, in_=ps32[:, 2560:2816].rearrange("p (h d) -> p h d", d=64), axis=AX.X, op=AL.add), waits=["ac"], incs=["dv"])
                    p.op("scalar", lambda e: e.activation(out=RED4, in_=RED4, func=AF.Sqrt, bias=cb[:, 1:2], scale=1.0 / 64.0), waits=["dv"], incs=["ac"])
                    p.op("vector", lambda e: e.reciprocal(KRS, RED4), waits=["ac"], incs=["dv"])
                    p.op("vector", lambda e: e.tensor_tensor(out=kv16[:, 0:256].rearrange("p (h d) -> p h d", d=64), in0=kv16[:, 0:256].rearrange("p (h d) -> p h d", d=64), in1=KRS.to_broadcast([128, 4, 64]), op=AL.mult), waits=["dv"], incs=["dv"])
                    p.op("vector", lambda e: e.tensor_tensor(out=kv16[:, 0:256].rearrange("p (h d) -> p h d", d=64), in0=kv16[:, 0:256].rearrange("p (h d) -> p h d", d=64), in1=KNB[:, :].rearrange("p (x d) -> p x d", x=1).to_broadcast([128, 4, 64]), op=AL.mult), waits=["dv"], incs=["dv"])
                    if dbg and t == 0:
                        p.op("gpsimd", lambda e: e.dma_start(out=dbg_t["dqh"], in_=xn_bf[:, :]), waits=["dv"], incs=["dbg"], dma=True)
                        p.op("gpsimd", lambda e: e.dma_start(out=dbg_t["dkh"], in_=kv16[:, 0:256]), waits=["dv"], incs=["dbg"], dma=True)
                        p.op("vector", lambda e: e.engine_nop(), waits=["dbg"], incs=["dv"])
                    p.op("scalar", lambda e, t=t: e.copy(V[:, t * 256:(t + 1) * 256], kv16[:, 256:512]), waits=["ac"], incs=["ac"])
                    dv_qk = p.cnt["dv"]

                    for c in range(8):
                        p.op("tensor", lambda e, c=c: e.transpose(psbf[:, c * 128:(c + 1) * 128], xn_bf[:, c * 128:(c + 1) * 128], identb[:, :]), waits=(["ac", ("dv", dv_qk)] if c == 0 else []), incs=(["pe"] if c == 7 else []))
                    qT = qTb[t % 2]
                    for h2 in range(H):
                        srch = (h2 % 2) * 64
                        dsth = ((h2 // 4) % 2) * 64
                        bq_ = (h2 % 4) + 4 * (h2 // 8)
                        p.op("scalar", lambda e, qT=qT, h2=h2, srch=srch, dsth=dsth, bq_=bq_: e.copy(qT[dsth:dsth + 64, bq_ * 128:(bq_ + 1) * 128], psbf[srch:srch + 64, (h2 // 2) * 128:(h2 // 2) * 128 + 128]), waits=["pe"], incs=(["ac"] if h2 == H - 1 else []))
                    for b2 in range(2):
                        p.op("tensor", lambda e, b2=b2: e.transpose(psbf[:, b2 * 128:(b2 + 1) * 128], kv16[:, b2 * 128:(b2 + 1) * 128], identb[:, :]), waits=(["ac", ("dv", dv_qk)] if b2 == 0 else []), incs=(["pe"] if b2 == 1 else []))
                    for b2 in range(2):
                        for half in range(2):
                            p.op("scalar", lambda e, b2=b2, half=half, t=t: e.copy(kt[b2][half * 64:(half + 1) * 64, t * 128:(t + 1) * 128], psbf[half * 64:(half + 1) * 64, b2 * 128:(b2 + 1) * 128]), waits=["pe"], incs=(["ac"] if (b2 == 1 and half == 1) else []))
                return prep_b

            prep(0)()
            pend = {"tail": None, "ffnb": None, "ldx": None, "dvl": None, "pb": None}
            hs = {"ac_exp": None, "ac_sc": None, "dv_mask": 0, "pending_av": None}
            for t in range(nt):
                slot = t % 4
                L = 128 * (t + 1)
                rslot = res[:, slot * D:(slot + 1) * D]

                nkc = L // 128
                qT = qTb[t % 2]
                hs["pending_av"] = None

                def stage_mm(h, t2, L2, qT2):
                    g2 = h // 4
                    ktile = kt[g2 // 2]
                    kbase = (g2 % 2) * 64
                    qslice = qT2[((h // 4) % 2) * 64:((h // 4) % 2) * 64 + 64, ((h % 4) + 4 * (h // 8)) * 128:((h % 4) + 4 * (h // 8)) * 128 + 128]
                    slope_h = float((2.0 ** (-(np.arange(H, dtype=np.float64) / (H - 1)) * 8.0))[h])
                    p.op("vector", lambda e, t2=t2, slope_h=slope_h, h=h: e.tensor_scalar(out=RB[:, h % 2:h % 2 + 1], in0=icol[:, 0:1], scalar1=float(t2 * 128), scalar2=-slope_h, op0=AL.add, op1=AL.mult), waits=["ac"], incs=["dv"])
                    off1 = max(0, L2 - pass_cap)
                    wins = [(off1, L2 - off1)] + ([(0, off1)] if off1 > 0 else [])
                    pcs = []
                    off, W = wins[0]
                    first = True
                    n0 = 0
                    while n0 < W:
                        nn = min(512, W - n0)
                        w0 = ((["ac"] if hs["ac_exp"] is None else [("ac", hs["ac_exp"])]) if first else [])
                        first = False
                        p.op("tensor", lambda e, n0=n0, nn=nn, off=off, ktile=ktile, kbase=kbase, qslice=qslice: e.matmul(ps32[:, n0:n0 + nn], qslice, ktile[kbase:kbase + 64, off + n0:off + n0 + nn], start=True, stop=False), waits=w0)
                        n0 += nn
                    n0 = 0
                    while n0 < W:
                        nn = min(512, W - n0)
                        last = (n0 + nn >= W)
                        p.op("tensor", lambda e, n0=n0, nn=nn, off=off, h=h: e.matmul(ps32[:, n0:n0 + nn], sl4[:, h * 128:(h + 1) * 128], j4b[:, off + n0:off + n0 + nn], start=False, stop=True), incs=(["pe"] if last else []))
                        n0 += nn
                    pe_sc0 = p.cnt["pe"]
                    dloc = (L2 - 128) - off
                    p.op("vector", lambda e, dloc=dloc: e.tensor_tensor(out=ps32[:, dloc:dloc + 128], in0=ps32[:, dloc:dloc + 128], in1=tri[:, :], op=AL.add), waits=[("pe", pe_sc0)], incs=["dv"])
                    dv_diag = p.cnt["dv"]
                    return wins, pe_sc0, qslice, ktile, kbase, dv_diag

                def stage_cp(h, st, L2):
                    wins, pe_sc0, qslice, ktile, kbase, dv_diag = st
                    dv_m = max(hs["dv_mask"], dv_diag)
                    off, W = wins[0]
                    p.op("scalar", lambda e, off=off, W=W, h=h: e.activation(out=s_sb[:, off:off + W], in_=ps32[:, 0:W], func=AF.Identity, bias=RB[:, h % 2:h % 2 + 1], scale=1.0), waits=[("pe", pe_sc0), ("dv", dv_m)], incs=["ac"])
                    sc0 = p.cnt["ac"]
                    p.op("scalar", lambda e, off=off, W=W, h=h: e.activation(out=Wb[:, off:off + W], in_=ps32[:, 0:W], func=AF.Exp, bias=RB[:, h % 2:h % 2 + 1], scale=1.0), incs=["ac"])
                    sc1 = p.cnt["ac"]
                    if len(wins) > 1:
                        off, W = wins[1]
                        first = True
                        n0 = 0
                        while n0 < W:
                            nn = min(512, W - n0)
                            w0 = [("ac", sc1)] if first else []
                            first = False
                            p.op("tensor", lambda e, n0=n0, nn=nn, off=off, ktile=ktile, kbase=kbase, qslice=qslice: e.matmul(ps32[:, n0:n0 + nn], qslice, ktile[kbase:kbase + 64, off + n0:off + n0 + nn], start=True, stop=False), waits=w0)
                            n0 += nn
                        n0 = 0
                        while n0 < W:
                            nn = min(512, W - n0)
                            last = (n0 + nn >= W)
                            p.op("tensor", lambda e, n0=n0, nn=nn, off=off, h=h: e.matmul(ps32[:, n0:n0 + nn], sl4[:, h * 128:(h + 1) * 128], j4b[:, off + n0:off + n0 + nn], start=False, stop=True), incs=(["pe"] if last else []))
                            n0 += nn
                        pe_sc1 = p.cnt["pe"]
                        p.op("scalar", lambda e, off=off, W=W, h=h: e.activation(out=s_sb[:, off:off + W], in_=ps32[:, 0:W], func=AF.Identity, bias=RB[:, h % 2:h % 2 + 1], scale=1.0), waits=[("pe", pe_sc1)], incs=["ac"])
                        hs["ac_sc"] = p.cnt["ac"]
                        p.op("scalar", lambda e, off=off, W=W, h=h: e.activation(out=Wb[:, off:off + W], in_=ps32[:, 0:W], func=AF.Exp, bias=RB[:, h % 2:h % 2 + 1], scale=1.0), incs=["ac"])
                    else:
                        hs["ac_sc"] = sc0
                    hs["ac_exp"] = p.cnt["ac"]

                if ph >= 2 and slot == 0:
                    stage_cp(0, stage_mm(0, t, L, qT), L)
                for h in (range(H) if ph >= 2 else []):
                    g = h // 4
                    for r in range(8):
                        p.op("vector", lambda e, r=r, L=L: e.max(out=T8[:, 8 * r:8 * r + 8], in_=s_sb[:, 0:L]), waits=(([("ac", hs["ac_sc"])] if hs["ac_sc"] else ["ac"]) + ["dv"] if r == 0 else ["dv"]), incs=["dv"])
                        p.op("vector", lambda e, r=r, L=L: e.match_replace(out=s_sb[:, 0:L], in_to_replace=T8[:, 8 * r:8 * r + 8], in_values=s_sb[:, 0:L], imm_value=RIMM), waits=["dv"], incs=["dv"])
                    if h + 1 < H:
                        st_next = (h + 1, stage_mm(h + 1, t, L, qT), L)
                    elif t + 1 < nt and slot != 3:
                        st_next = (0, stage_mm(0, t + 1, L + 128, qTb[(t + 1) % 2]), L + 128)
                    else:
                        st_next = None
                    p.op("vector", lambda e, L=L: e.scalar_tensor_tensor(out=Wn[:, 0:L], in0=s_sb[:, 0:L], scalar=FLAGT, in1=Wb[:, 0:L], op0=AL.is_le, op1=AL.mult, accum_out=Zc), waits=["dv", ("ac", hs["ac_exp"]), "wt%d" % ((h - 1) % 8)], incs=["dv"])
                    hs["dv_mask"] = p.cnt["dv"]
                    p.op("vector", lambda e: e.reciprocal(RZ, Zc), waits=["dv"], incs=["dv"])
                    p.op("vector", lambda e, L=L: e.tensor_scalar(out=Wn[:, 0:L], in0=Wn[:, 0:L], scalar1=RZ, scalar2=None, op0=AL.mult), waits=["dv"], incs=["dv"])
                    if st_next is not None:
                        stage_cp(st_next[0], st_next[1], st_next[2])
                    if h == 1 and pend["tail"] is not None:
                        pend["ffnb"] = pend["tail"]()
                        pend["tail"] = None
                    if hs.get("pending_av") is not None:
                        hs["pending_av"]()
                        hs["pending_av"] = None
                    if h == 2:
                        if pend["ffnb"] is not None:
                            pend["ffnb"]()
                            pend["ffnb"] = None
                        if slot != 3 and t + 1 < nt:
                            p.op("sync", lambda e, t=t: e.dma_start(out=res[:, ((t + 1) % 4) * D:((t + 1) % 4 + 1) * D], in_=x_d[(t + 1) * 128:(t + 2) * 128, :]), waits=["out"], incs=["ld"], dma=True)
                            pend["ldx"] = p.cnt["ld"]
                            pend["dvl"] = p.cnt["dv"]
                    if h == 8 and slot != 3 and t + 1 < nt:
                        pend["pb"] = prep(t + 1, ldx=pend["ldx"], dvl=pend["dvl"])
                    if h == 9 and pend.get("pb") is not None:
                        pend["pb"]()
                        pend["pb"] = None
                    if dbg and t == 0 and h == 0:
                        p.op("gpsimd", lambda e: e.dma_start(out=dbg_t["dT8"], in_=T8[:, :]), waits=["dv"], incs=["dbg"], dma=True)
                        p.op("gpsimd", lambda e: e.dma_start(out=dbg_t["dwn"], in_=Wn[:, 0:128]), waits=["dv"], incs=["dbg"], dma=True)
                        p.op("vector", lambda e: e.engine_nop(), waits=["dbg"], incs=["dv"])
                    wch = "wt%d" % (h % 8)
                    if ph < 3:
                        continue
                    for kc in range(nkc):
                        p.op("sync", lambda e, kc=kc: e.dma_start_transpose(WT[:, kc * 128:(kc + 1) * 128], Wn[:, kc * 128:(kc + 1) * 128]), waits=["dv", "pe"], incs=[wch], dma=True)

                    def av_emit(h=h, g=g, nkc=nkc, wch=wch):
                        for kc in range(nkc):
                            w0 = [wch, "ac"] if kc == 0 else []
                            p.op("tensor", lambda e, kc=kc, g=g, nkc=nkc: e.matmul(ps32[0:64, 2048:2176], V[:, kc * 256 + g * 64: kc * 256 + (g + 1) * 64], WT[:, kc * 128:(kc + 1) * 128], start=(kc == 0), stop=(kc == nkc - 1)), waits=w0, incs=(["pe"] if kc == nkc - 1 else []))
                        p.op("scalar", lambda e, h=h: e.copy(CTXT[(h % 2) * 64:(h % 2) * 64 + 64, (h // 2) * 128:(h // 2) * 128 + 128], ps32[0:64, 2048:2176]), waits=["pe"], incs=["ac"])
                    hs["pending_av"] = av_emit

                if hs.get("pending_av") is not None:
                    hs["pending_av"]()
                    hs["pending_av"] = None

                def make_tail(t=t, rslot=rslot):
                    ac_t = p.cnt["ac"]
                    dv_t = p.cnt["dv"]
                    def tail():
                        for c in (range(8) if ph >= 4 else []):
                            w0 = [("ac", ac_t), ("dv", dv_t)] if c == 0 else []
                            p.op("tensor", lambda e, c=c: e.matmul(ps32[:, 2560:3072], CTXT[:, c * 128:(c + 1) * 128], wob[:, c * D:c * D + 512], start=(c == 0), stop=False), waits=w0)
                            p.op("tensor", lambda e, c=c: e.matmul(ps32[:, 3072:3584], CTXT[:, c * 128:(c + 1) * 128], wob[:, c * D + 512:(c + 1) * D], start=(c == 0), stop=False))
                        fb = None
                        if ph >= 4:
                            p.op("tensor", lambda e: e.matmul(ps32[:, 2560:3072], onescb, bor[:, 0:512], start=False, stop=True))
                            p.op("tensor", lambda e: e.matmul(ps32[:, 3072:3584], onescb, bor[:, 512:1024], start=False, stop=True), incs=["pe"])
                            if dbg and t == 0:
                                p.op("gpsimd", lambda e: e.dma_start(out=dbg_t["dctx"], in_=CTXT[:, :]), waits=["ac"], incs=["dbg"], dma=True)
                                p.op("vector", lambda e: e.engine_nop(), waits=["dbg"], incs=["dv"])
                            p.op("vector", lambda e, rslot=rslot: e.scalar_tensor_tensor(out=rslot, in0=ps32[:, 2560:3584], scalar=1.0, in1=rslot, op0=AL.mult, op1=AL.add), waits=["pe", "dv"], incs=["dv"])
                            if dbg and t == 0:
                                p.op("gpsimd", lambda e: e.dma_start(out=dbg_t["dres1"], in_=rslot), waits=["dv"], incs=["dbg"], dma=True)
                                p.op("vector", lambda e: e.engine_nop(), waits=["dbg"], incs=["dv"])
                        dv_ae = p.cnt["dv"]
                        if ph >= 5:
                            fb = ffn(rslot, tap=(dbg and t == 0), defer_lora=True, dv_ae=dv_ae)
                        if dbg and t == 0:
                            p.op("gpsimd", lambda e: e.dma_start(out=dbg_t["dres2"], in_=rslot), waits=["dv"], incs=["dbg"], dma=True)
                            p.op("vector", lambda e: e.engine_nop(), waits=["dbg"], incs=["dv"])
                        return fb
                    return tail

                if slot == 3 or ph < 2:
                    fb = make_tail()()
                    if fb is not None:
                        fb()
                    if ph < 2 and slot != 3 and t + 1 < nt:
                        prep(t + 1)()
                else:
                    pend["tail"] = make_tail()

                if slot == 3:
                    G = t // 4
                    for tt in (range(4) if ph >= 6 else []):
                        rs = res[:, tt * D:(tt + 1) * D]
                        p.op("vector", lambda e, rs=rs: e.reduce_sum(out=S1, in_=rs, axis=AX.X), waits=["dv", "ac"], incs=["dv"])
                        p.op("scalar", lambda e, rs=rs: e.activation(out=ps32[:, 2560:3584], in_=rs, func=AF.Square, accum_out=SQ), waits=["ac", "dv"], incs=["ac"])
                        p.op("vector", lambda e: e.tensor_scalar(out=MEAN, in0=S1, scalar1=1.0 / D, scalar2=None, op0=AL.mult), waits=["dv"], incs=["dv"])
                        p.op("vector", lambda e: e.tensor_scalar(out=MSQ, in0=SQ, scalar1=1.0 / D, scalar2=None, op0=AL.mult), waits=["ac", "dv"], incs=["dv"])
                        p.op("vector", lambda e: e.tensor_mul(M2, MEAN, MEAN), waits=["dv"], incs=["dv"])
                        p.op("vector", lambda e: e.tensor_sub(VAR, MSQ, M2), waits=["dv"], incs=["dv"])
                        p.op("scalar", lambda e: e.activation(out=STD, in_=VAR, func=AF.Sqrt, bias=cb[:, 0:1], scale=1.0), waits=["dv"], incs=["ac"])
                        p.op("vector", lambda e: e.reciprocal(RSTD, STD), waits=["ac"], incs=["dv"])
                        p.op("vector", lambda e: e.tensor_mul(T2, VAR, RSTD), waits=["dv"], incs=["dv"])
                        p.op("vector", lambda e: e.tensor_mul(T2, T2, RSTD), waits=["dv"], incs=["dv"])
                        p.op("scalar", lambda e: e.activation(out=T0, in_=T2, func=AF.Sqrt, bias=cb[:, 0:1], scale=1.0), waits=["dv"], incs=["ac"])
                        p.op("vector", lambda e: e.reciprocal(T2, T0), waits=["ac"], incs=["dv"])
                        p.op("vector", lambda e, tt=tt: e.tensor_scalar(out=ONEP[:, tt:tt + 1], in0=RSTD, scalar1=1.0, scalar2=None, op0=AL.add), waits=["dv"], incs=["dv"])
                        p.op("vector", lambda e, tt=tt: e.tensor_mul(NMR[:, tt:tt + 1], MEAN, RSTD), waits=["dv"], incs=["dv"])
                        p.op("vector", lambda e, tt=tt: e.tensor_scalar(out=NMR[:, tt:tt + 1], in0=NMR[:, tt:tt + 1], scalar1=-1.0, scalar2=None, op0=AL.mult), waits=["dv"], incs=["dv"])
                        p.op("vector", lambda e: e.tensor_mul(T0, RSTD, T2), waits=["dv"], incs=["dv"])
                        p.op("vector", lambda e: e.tensor_mul(NEGB, MEAN, T0), waits=["dv"], incs=["dv"])
                        p.op("vector", lambda e: e.tensor_scalar(out=NEGB, in0=NEGB, scalar1=-1.0, scalar2=None, op0=AL.mult), waits=["dv"], incs=["dv"])
                        p.op("scalar", lambda e, rs=rs: e.activation(out=xn_bf[:, :], in_=rs, func=AF.Identity, bias=NEGB, scale=T0), waits=["dv"], incs=["ac"])
                        for c in range(8):
                            p.op("tensor", lambda e, c=c: e.transpose(psbf[:, c * 128:(c + 1) * 128], xn_bf[:, c * 128:(c + 1) * 128], identb[:, :]), waits=["ac", "dv"], incs=(["pe"] if c == 7 else []))
                        for c in range(8):
                            p.op("scalar", lambda e, c=c, tt=tt: e.copy(xn3T[:, c * 512 + tt * 128: c * 512 + (tt + 1) * 128], psbf[:, c * 128:(c + 1) * 128]), waits=["pe"], incs=(["ac"] if c == 7 else []))
                    for ncx in (range(8) if ph >= 6 else []):
                        b0 = ((2 * ncx) % 3) * 2 * D
                        b1 = ((2 * ncx + 1) % 3) * 2 * D
                        for ph01, wis in ((0, (0, 3)), (1, (1, 2))):
                            slot = (2 * ncx + ph01) % 3
                            base = slot * 2 * D
                            for k2, wi in enumerate(wis):
                                p.op("sync", lambda e, wi=wi, ncx=ncx, base=base, k2=k2, slot=slot: e.dma_start(
                                    out=ssw[:, base + k2 * D: base + (k2 + 1) * D].rearrange("p (dc c) -> p dc c", c=128),
                                    in_=wt_scr[wi][:, ncx * 128:(ncx + 1) * 128].rearrange("(dc p) c -> p dc c", p=128)), waits=[("pe", ssw_war[slot]), ("ssw", p0_state["ssw"])], incs=["sw%d" % slot], dma=True)
                        for k2 in range(2):
                            for dc in range(8):
                                w0 = ["sw%d" % ((2 * ncx) % 3), "ac", "dv"] if (k2 == 0 and dc == 0) else []
                                reg = 2560 if k2 == 0 else 3072
                                p.op("tensor", lambda e, k2=k2, dc=dc, b0=b0, reg=reg: e.matmul(ps32[:, reg:reg + 512], ssw[:, b0 + k2 * D + dc * 128: b0 + k2 * D + (dc + 1) * 128], xn3T[:, dc * 512:(dc + 1) * 512], start=(dc == 0), stop=(dc == 7)), waits=w0, incs=(["pe"] if (k2 == 1 and dc == 7) else []))
                        ssw_war[(2 * ncx) % 3] = p.cnt["pe"]
                        p.op("scalar", lambda e, ncx=ncx: e.activation(out=asg[:, :], in_=ps32[:, 2560:3072], func=AF.Sigmoid, bias=abt[:, ncx:ncx + 1], scale=1.0), waits=["pe"], incs=["ac"])
                        p.op("scalar", lambda e: e.activation(out=dsg[:, :], in_=ps32[:, 3072:3584], func=AF.Sigmoid), incs=["ac"])
                        for k2 in range(2):
                            for dc in range(8):
                                w0 = ["sw%d" % ((2 * ncx + 1) % 3), ("ac", p.cnt["ac"])] if (k2 == 0 and dc == 0) else []
                                reg = 2560 if k2 == 0 else 3072
                                p.op("tensor", lambda e, k2=k2, dc=dc, b1=b1, reg=reg: e.matmul(ps32[:, reg:reg + 512], ssw[:, b1 + k2 * D + dc * 128: b1 + k2 * D + (dc + 1) * 128], xn3T[:, dc * 512:(dc + 1) * 512], start=(dc == 0), stop=(dc == 7)), waits=w0, incs=(["pe"] if (k2 == 1 and dc == 7) else []))
                        ssw_war[(2 * ncx + 1) % 3] = p.cnt["pe"]
                        p.op("scalar", lambda e: e.activation(out=csg[:, :], in_=ps32[:, 3072:3584], func=AF.Sigmoid), waits=["pe"], incs=["ac"])
                        p.op("vector", lambda e: e.tensor_mul(dbt[:, :], ps32[:, 2560:3072], dsg[:, :]), waits=["ac", "dv"], incs=["dv"])
                        p.op("vector", lambda e, ncx=ncx: e.tensor_tensor_scan(out=hh[:, :], data0=asg[:, :], data1=dbt[:, :], initial=carry[:, ncx:ncx + 1], op0=AL.mult, op1=AL.add), waits=["dv"], incs=["dv"])
                        p.op("vector", lambda e, ncx=ncx: e.tensor_copy(carry[:, ncx:ncx + 1], hh[:, 511:512]), waits=["dv"], incs=["dv"])
                        p.op("vector", lambda e, ncx=ncx: e.tensor_mul(Hp[:, ncx * 512:(ncx + 1) * 512], csg[:, :], hh[:, :]), waits=["dv"], incs=["dv"])
                    for tt in range(4):
                        rs = res[:, tt * D:(tt + 1) * D]
                        for ncx in (range(8) if ph >= 6 else []):
                            w0 = ["dv", "ac"] if ncx == 0 else []
                            p.op("tensor", lambda e, ncx=ncx, tt=tt: e.matmul(ps32[:, 2560:3072], Hp[:, ncx * 512 + tt * 128: ncx * 512 + (tt + 1) * 128], outwT[:, ncx * 1024: ncx * 1024 + 512], start=(ncx == 0), stop=False), waits=w0)
                            p.op("tensor", lambda e, ncx=ncx, tt=tt: e.matmul(ps32[:, 3072:3584], Hp[:, ncx * 512 + tt * 128: ncx * 512 + (tt + 1) * 128], outwT[:, ncx * 1024 + 512:(ncx + 1) * 1024], start=(ncx == 0), stop=False))
                        if ph >= 6:
                            p.op("tensor", lambda e: e.matmul(ps32[:, 2560:3072], onescb, obr[:, 0:512], start=False, stop=True))
                            p.op("tensor", lambda e: e.matmul(ps32[:, 3072:3584], onescb, obr[:, 512:1024], start=False, stop=True), incs=["pe"])
                            p.op("vector", lambda e, rs=rs, tt=tt: e.tensor_scalar(out=rs, in0=rs, scalar1=ONEP[:, tt:tt + 1], scalar2=NMR[:, tt:tt + 1], op0=AL.mult, op1=AL.add), waits=["pe", "dv"], incs=["dv"])
                            p.op("vector", lambda e, rs=rs: e.scalar_tensor_tensor(out=rs, in0=ps32[:, 2560:3584], scalar=1.0, in1=rs, op0=AL.mult, op1=AL.add), waits=["dv"], incs=["dv"])
                        if dbg and G == 0 and tt == 0:
                            p.op("gpsimd", lambda e, rs=rs: e.dma_start(out=dbg_t["dres3"], in_=rs), waits=["dv"], incs=["dbg"], dma=True)
                            p.op("vector", lambda e: e.engine_nop(), waits=["dbg"], incs=["dv"])
                        if ph >= 7:
                            ffn(rs)
                        p.op("sync", lambda e, G=G, tt=tt, rs=rs: e.dma_start(out=y_d[(G * 4 + tt) * 128:(G * 4 + tt + 1) * 128, :], in_=rs), waits=["dv"], incs=["out"], dma=True)
                    if t + 1 < nt:
                        prep(t + 1)()

        for name, deco in [("sync", block.sync), ("tensor", block.tensor),
                           ("scalar", block.scalar), ("vector", block.vector),
                           ("gpsimd", block.gpsimd)]:
            def body(eng, name=name):
                p.begin(name, eng)
                sched()
                if name == "sync":
                    eng.wait_ge(sems["out"], nt * 16)
            deco(body)

    return nc


_NC_CACHE = None


def kernel(**inputs):
    global _NC_CACHE
    xs = np.ascontiguousarray(np.asarray(inputs["L0_attn_a_x"], dtype=np.float32))
    B = xs.shape[0]
    wmap = {
        "wq": "L0_attn_a_wq", "bq": "L0_attn_a_bq", "wkv": "L0_attn_a_wkv",
        "bkv": "L0_attn_a_bkv", "wo": "L0_attn_a_wo", "bo": "L0_attn_a_bo",
        "qn": "L0_attn_a_qn_w", "kn": "L0_attn_a_kn_w", "nw": "L0_attn_b_norm_w",
        "aw": "L0_attn_b_A_w", "abias": "L0_attn_b_A_b", "bw": "L0_attn_b_B_w",
        "cw": "L0_attn_b_C_w", "dw": "L0_attn_b_D_w", "ow": "L0_attn_b_out_w",
        "obias": "L0_attn_b_out_b", "fk": "L0_ffn_kernel", "fb": "L0_ffn_bias",
        "fa": "L0_ffn_a", "fbm": "L0_ffn_b",
    }
    shared = {k: np.ascontiguousarray(np.asarray(inputs[v], dtype=np.float32))
              for k, v in wmap.items()}
    if _NC_CACHE is None:
        _NC_CACHE = build_nc()
    nc = _NC_CACHE
    in_maps = [dict(shared, x=np.ascontiguousarray(xs[b])) for b in range(B)]
    r = run_bass_kernel_spmd(nc, in_maps, core_ids=list(range(B)))
    out = np.stack([np.asarray(r.results[b]["y"], dtype=np.float32) for b in range(B)])
    return out
